# version 25
# baseline (speedup 1.0000x reference)
import math
import contextlib
import numpy as np
import ml_dtypes
import concourse.bass as bass
import concourse.mybir as mybir
from concourse.bass_utils import run_bass_kernel_spmd

F32 = mybir.dt.float32
BF16 = mybir.dt.bfloat16
AX = mybir.AxisListType
ALU = mybir.AluOpType
ACT = mybir.ActivationFunctionType
bf16 = ml_dtypes.bfloat16

B, T, C, H, D, HID = 4, 4096, 1024, 16, 64, 4096
NT = 2048
SUBS = ([0, 3, 4, 7, 8, 11, 12, 15], [1, 2, 5, 6, 9, 10, 13, 14])
NEG = -1.0e30
KTJ = [j // 2 + 1 for j in range(16)]

ENGINES = ("tensor", "vector", "scalar", "gpsimd", "sync")
SEMS = ["t", "v", "s", "bar", "ga", "sa", "sb", "mv", "mw", "yw0", "yw1", "sw",
        "gc", "sr", "gx", "wa0", "wa1", "wh0", "wh1", "wc0", "wc1", "sx0", "sx1", "out",
        "sl0", "sl1", "sl2", "sl3", "sl4", "yt0", "yt1", "yt2", "rt0", "rt1", "rt2",
        "xo0", "xo1", "xs0", "xs1", "bv", "ka0", "ka1", "qa0", "qa1", "va0", "va1"]

_DEV_TAPS = ()


class Prog:

    def __init__(self):
        self.steps = {e: [] for e in ENGINES}
        self.cnt = {sname: 0 for sname in SEMS}
        self.gate = {}

    def op(self, eng, fn, waits=(), inc=None, n=1):
        post = None
        if inc is not None:
            self.cnt[inc] += n
            post = self.cnt[inc]
        self.steps[eng].append((fn, list(waits), inc, n))
        return post

    def dma(self, eng, fn, waits=(), inc=None):
        return self.op(eng, fn, waits, inc, n=16)

    def wait(self, eng, waits):
        self.steps[eng].append((None, list(waits), None, 0))

    ENG_SEMS = ("t", "v", "s")

    def g(self, *names):
        out = []
        for nm in names:
            gt = self.gate.get(nm)
            if gt is None:
                continue
            sem, val = gt
            if sem in self.ENG_SEMS:
                out.append((sem, val))
            else:
                out.append((sem, self.cnt[sem]))
        return out

    def setg(self, nm, sem, val):
        self.gate[nm] = (sem, val)

    def barrier(self):
        base = self.cnt["bar"]
        for e in ENGINES:
            if e == "sync":
                self.op(e, lambda eng: eng.nop(), inc="bar")
            else:
                self.op(e, lambda eng: eng.drain(), inc="bar")
        for e in ENGINES:
            self.wait(e, [("bar", base + len(ENGINES))])

    def emit(self, block, sems):
        def mk(ename):
            def body(eng):
                for fn, waits, inc, n in self.steps[ename]:
                    for (sname, val) in waits:
                        eng.wait_ge(sems[sname], val)
                    if fn is None:
                        continue
                    ins = fn(eng)
                    if inc is not None:
                        ins.then_inc(sems[inc], n)
            return body
        for ename in ENGINES:
            if self.steps[ename]:
                getattr(block, ename)(mk(ename))


def _build():
    nc = bass.Bass("TRN2", target_bir_lowering=False)
    P = Prog()

    def inp(name, shape, dtype=F32):
        return nc.dram_tensor(name, list(shape), dtype, kind="ExternalInput")

    def scratch(name, shape, dtype):
        kind = "ExternalOutput" if name in _DEV_TAPS else "Internal"
        return nc.dram_tensor(name, list(shape), dtype, kind=kind)

    xT_b = inp("xT_b", [C, T])
    xT_own = inp("xT_own", [C, NT])
    x_own = inp("x_own", [NT, C])
    W_attn = inp("W_attn", [C, 3 * C])
    b_attn = inp("b_attn", [3 * C])
    W_proj = inp("W_proj", [C, C])
    b_proj = inp("b_proj", [C])
    temp = inp("temperature", [1])
    w1T = inp("w1T", [C, HID])
    w2T = inp("w2T", [C, HID])
    h1T = inp("h1T", [HID, HID])
    h2T = inp("h2T", [HID, HID])
    w3T = inp("w3T", [HID, C]); w3s = inp("w3s", [C])
    scales_in = inp("scales_in", [128, 4, 32])
    bqk_in = inp("bqk_in", [128, 2, 8])
    ident = inp("ident", [128, 128], BF16)
    masks1 = inp("masks1", [16, 128, 512], BF16)
    masks2 = inp("masks2", [4, 8, 128, 512], BF16)
    biasrows = inp("biasrows", [H, 4, T], BF16)
    qinit = inp("qinit", [4, NT], BF16)

    out_h = nc.dram_tensor("out", [NT, C], F32, kind="ExternalOutput")

    K_scr = scratch("K_scr", [H, 64, T], BF16)
    Q_scr = scratch("Q_scr", [H, 64, NT], BF16)
    V_scr = scratch("V_scr", [T, H * 65], BF16)
    x2_scr = scratch("x2_scr", [NT, C], F32)
    x2fm_scr = scratch("x2fm_scr", [8, 128, NT], BF16)
    vec_scr = scratch("vec_scr", [17, NT], F32)
    y_scr = scratch("y_scr", [64, H, NT], BF16)

    ctx = contextlib.ExitStack()

    def sbP(name, shape, dtype):
        return ctx.enter_context(nc.sbuf_tensor(name, list(shape), dtype))

    id_bf = sbP("id_bf", [128, 128], BF16)
    sbc = sbP("sbc", [128, 1], F32)
    bq_f = sbP("bq_f", [128, 8], F32)
    bq_s = sbP("bq_s", [128, 8], F32)
    bk_sb = sbP("bk_sb", [128, 8], F32)
    tstat = sbP("tstat", [1, 4], F32)
    srow_t = sbP("srow_t", [1, 128], F32)
    scales = sbP("scales", [128, 4, 32], F32)
    psum_h = ctx.enter_context(nc.psum_tensor("ps", [128, 4096], F32))
    psum = psum_h
    psum_bf = psum_h.bitcast(BF16)

    def bank(i, p0=0, p1=128, off=0, n=512):
        return psum[p0:p1, 512 * i + off:512 * i + off + n]

    def bwait(*banks):
        return P.g(*[f"bank{i}" for i in banks])

    def bset(i, sem, val):
        P.setg(f"bank{i}", sem, val)

    ctxA = contextlib.ExitStack()

    def sbA(name, shape, dtype):
        return ctxA.enter_context(nc.sbuf_tensor(name, list(shape), dtype))

    x_fm = sbA("x_fm", [128, 8, T], BF16)
    xo_fm = sbA("xo_fm", [128, 8, NT], BF16)
    Wa_sb = sbA("Wa_sb", [128, 8, 3 * C], BF16)
    Kstag = [sbA(f"Kstag{i}", [128, T], BF16) for i in range(2)]
    Qstag = [sbA(f"Qstag{i}", [128, NT], BF16) for i in range(2)]
    Vstag = [sbA(f"Vstag{i}", [128, H * 65], BF16) for i in range(2)]
    bv_bc = sbA("bv_bc", [128, C], F32)

    for c in range(8):
        P.dma("gpsimd", lambda g_, c=c: g_.dma_start(
            out=x_fm[:, c, :],
            in_=xT_b[128 * c:128 * c + 128, :]), inc="ga")
        ga_loads = P.dma("gpsimd", lambda g_, c=c: g_.dma_start(
            out=Wa_sb[:, c, :],
            in_=W_attn[128 * c:128 * c + 128, :]), inc="ga")
    xo_loads = 0
    for c in range(8):
        xo_loads = P.dma("gpsimd", lambda g_, c=c: g_.dma_start(
            out=xo_fm[:, c, :],
            in_=xT_own[128 * c:128 * c + 128, :]), inc="mv")
    P.dma("sync", lambda sy: sy.dma_start(out=scales[:], in_=scales_in[:]), inc="sa")
    P.dma("sync", lambda sy: sy.dma_start(out=tstat[0:1, 0:1], in_=temp[None, :]), inc="sa")
    P.dma("sync", lambda sy: sy.dma_start(out=bq_f[:], in_=bqk_in[:, 0, :]), inc="sa")
    P.dma("sync", lambda sy: sy.dma_start(out=bk_sb[:], in_=bqk_in[:, 1, :]), inc="sa")
    sa = P.dma("sync", lambda sy: sy.dma_start(out=id_bf[:], in_=ident[:]), inc="sa")
    v = P.op("vector", lambda e: e.tensor_scalar_mul(tstat[0:1, 1:2], tstat[0:1, 0:1], 8.0),
             waits=[("sa", sa)], inc="v")
    v = P.op("vector", lambda e: e.reciprocal(tstat[0:1, 2:3], tstat[0:1, 1:2]),
             waits=[("v", v)], inc="v")
    v = P.op("vector", lambda e: e.memset(srow_t[:], 1.0), waits=[("v", v)], inc="v")
    v = P.op("vector", lambda e: e.tensor_scalar_mul(
        srow_t[:], srow_t[:], tstat[0:1, 2:3]), waits=[("v", v)], inc="v")
    sa = P.dma("sync", lambda sy: sy.dma_start(
        out=vec_scr[16:17, 0:128], in_=srow_t[:]), waits=[("v", v)], inc="sa")
    sa_sbc = P.dma("sync", lambda sy: sy.dma_start(
        out=sbc[:], in_=vec_scr[16:17, 0:128].rearrange("a (p b) -> (a p) b", p=128)),
        waits=[("sa", sa)], inc="sa")
    v_bqs = P.op("vector", lambda e: e.tensor_scalar_mul(bq_s[:], bq_f[:], sbc[:, 0:1]),
                 waits=[("sa", sa_sbc)], inc="v")
    sa_bv = P.dma("sync", lambda sy: sy.dma_start(
        out=bv_bc[:], in_=b_attn[None, 2 * C:3 * C].broadcast_to([128, C])), inc="bv")

    t = P.cnt["t"]; s = P.cnt["s"]

    for kind, n_j, stag, W_off, x_src, scr in (
            ("K", 8, Kstag, C, x_fm, K_scr), ("Q", 4, Qstag, 0, xo_fm, Q_scr)):
        for oc in range(8):
            stg = stag[oc % 2]
            stg_nm = f"{kind}stag{oc % 2}"
            st_sem = ("ka" if kind == "K" else "qa") + str(oc % 2)
            for j in range(n_j):
                bk_i = (oc * n_j + j) % 2
                for c in range(8):
                    wl = []
                    if c == 0:
                        if kind == "K" and oc == 0 and j == 0:
                            wl += [("ga", ga_loads), ("sa", sa_sbc)]
                        if kind == "Q" and oc == 0 and j == 0:
                            wl.append(("mv", xo_loads))
                        wl += bwait(bk_i)
                    t = P.op("tensor", lambda e, oc=oc, j=j, c=c, bk_i=bk_i,
                             W_off=W_off, x_src=x_src: e.matmul(
                        bank(bk_i),
                        Wa_sb[:, c, W_off + 128 * oc:W_off + 128 * oc + 128],
                        x_src[:, c, 512 * j:512 * j + 512],
                        start=(c == 0), stop=(c == 7)), waits=wl, inc="t")
                wl = [("t", t)]
                if j == 0:
                    wl += P.g(stg_nm)
                if kind == "Q":
                    wl.append(("v", v_bqs))
                    s = P.op("scalar", lambda e, oc=oc, j=j, bk_i=bk_i, stg=stg: e.activation(
                        stg[:, 512 * j:512 * j + 512], bank(bk_i), ACT.Identity,
                        bias=bq_s[:, oc:oc + 1], scale=sbc[:, 0:1]), waits=wl, inc="s")
                else:
                    s = P.op("scalar", lambda e, oc=oc, j=j, bk_i=bk_i, stg=stg: e.activation(
                        stg[:, 512 * j:512 * j + 512], bank(bk_i), ACT.Identity,
                        bias=bk_sb[:, oc:oc + 1], scale=1.0), waits=wl, inc="s")
                bset(bk_i, "s", s)
            for hh in range(2):
                sa2 = P.dma("sync", lambda sy, oc=oc, hh=hh, stg=stg, scr=scr: sy.dma_start(
                    out=scr[2 * oc + hh], in_=stg[64 * hh:64 * hh + 64, :]),
                    waits=[("s", s)], inc=st_sem)
            P.setg(stg_nm, st_sem, sa2)

    vst_init = []
    for i in range(2):
        vst_init.append(P.op("vector", lambda e, i=i: e.memset(
            Vstag[i].rearrange("p (h c) -> p h c", c=65)[:, :, 64:65], 1.0), inc="v"))
    vst_drain = P.op("vector", lambda e: e.drain(), inc="v")
    for jt in range(32):
        stg = Vstag[jt % 2]
        stg_nm = f"Vstag{jt % 2}"
        for nh in range(2):
            bk_i = 2 + nh
            for c in range(8):
                wl = bwait(bk_i) if c == 0 else []
                t = P.op("tensor", lambda e, jt=jt, nh=nh, c=c, bk_i=bk_i: e.matmul(
                    bank(bk_i),
                    x_fm[:, c, 128 * jt:128 * jt + 128],
                    Wa_sb[:, c, 2 * C + 512 * nh:2 * C + 512 * nh + 512],
                    start=(c == 0), stop=(c == 7)), waits=wl, inc="t")
            wl = [("t", t), ("bv", sa_bv)]
            if nh == 0:
                wl += P.g(stg_nm)
            v = P.op("vector", lambda e, nh=nh, bk_i=bk_i, stg=stg: e.tensor_tensor(
                stg.rearrange("p (h c) -> p h c", c=65)[:, 8 * nh:8 * nh + 8, 0:64],
                bank(bk_i).rearrange("p (h c) -> p h c", c=64),
                bv_bc[:, 512 * nh:512 * nh + 512], ALU.add),
                waits=wl + ([("v", vst_drain)] if jt == 0 and nh == 0 else []), inc="v")
            bset(bk_i, "v", v)
        sa2 = P.dma("sync", lambda sy, jt=jt, stg=stg: sy.dma_start(
            out=V_scr[128 * jt:128 * jt + 128, :], in_=stg[:]),
            waits=[("v", v)], inc="va" + str(jt % 2))
        P.setg(stg_nm, "va" + str(jt % 2), sa2)
    A_store_waits = [("ka0", P.cnt["ka0"]), ("ka1", P.cnt["ka1"]),
                     ("qa0", P.cnt["qa0"]), ("qa1", P.cnt["qa1"]),
                     ("va0", P.cnt["va0"]), ("va1", P.cnt["va1"])]

    P.barrier()
    ctxA.close()

    ctxBC = contextlib.ExitStack()
    sums_all = ctxBC.enter_context(nc.sbuf_tensor("sums_all", [16, NT], F32))

    ctxB = contextlib.ExitStack()

    def sbB(name, shape, dtype):
        return ctxB.enter_context(nc.sbuf_tensor(name, list(shape), dtype))

    NSLOT = 5
    Kh = [sbB(f"Kh{i}", [68, T], BF16) for i in range(NSLOT)]
    Qh = [sbB(f"Qh{i}", [68, NT], BF16) for i in range(NSLOT)]
    Vh = [sbB(f"Vh{i}", [128, 32, 65], BF16) for i in range(NSLOT)]
    m1_sb = sbB("m1_sb", [128, 16, 512], BF16)
    m2_sb = sbB("m2_sb", [128, 4, 8, 512], BF16)
    Mcols = sbB("Mcols", [128, 16], F32)
    Mtmp = sbB("Mtmp", [128, 2], F32)
    Mcols_bf = sbB("Mcols_bf", [128, 16], BF16)
    Mrow = sbB("Mrow", [1, NT], BF16)
    sums_stage = sbB("sums_stage", [65, NT], F32)
    PT = [sbB(f"PT{i}", [128, 1024], BF16) for i in range(2)]
    ystage = [sbB(f"ystage{i}", [64, 512], BF16) for i in range(2)]

    P.dma("sync", lambda sy: sy.dma_start(
        out=m1_sb[:], in_=masks1.rearrange("j p m -> p j m")),
        waits=A_store_waits, inc="sb")
    P.dma("sync", lambda sy: sy.dma_start(
        out=m2_sb[:], in_=masks2.rearrange("a b p m -> p a b m")), inc="sb")

    slot_loaded = {}
    vg_init = {}
    mv_M = {}

    def load_head(h):
        slot = h % NSLOT
        sls = f"sl{slot}"
        wl = P.g(f"slot{slot}")
        P.dma("sync", lambda sy, h=h, slot=slot: sy.dma_start(
            out=Kh[slot][0:64, :], in_=K_scr[h]), waits=wl, inc=sls)
        P.dma("sync", lambda sy, h=h, slot=slot: sy.dma_start(
            out=Kh[slot][64:68, :], in_=biasrows[h]), inc=sls)
        P.dma("sync", lambda sy, h=h, slot=slot: sy.dma_start(
            out=Qh[slot][0:64, :], in_=Q_scr[h]), inc=sls)
        P.dma("sync", lambda sy, slot=slot: sy.dma_start(
            out=Qh[slot][64:68, :], in_=qinit[:]), inc=sls)
        sbv = P.dma("sync", lambda sy, h=h, slot=slot: sy.dma_start(
            out=Vh[slot][:],
            in_=V_scr.rearrange("(kt p) (hh d) -> p kt hh d", p=128, d=65)[:, :, h, :]),
            inc=sls)
        slot_loaded[h] = sbv
        vg_init[h] = sbv

    def p1_gen(h):
        slot = h % NSLOT
        Q_, K_ = Qh[slot], Kh[slot]
        first = True
        vloc = None
        for j in range(16):
            n512 = KTJ[j]
            nch = (n512 + 1) // 2
            for chn in range(nch):
                tiles = list(range(2 * chn, min(2 * chn + 2, n512)))
                t = None
                for idx, kt5 in enumerate(tiles):
                    wl = []
                    if idx == 0:
                        wl += bwait(0, 1)
                        if first:
                            wl += [(f"sl{slot}", slot_loaded[h]), ("sb", 32)] + P.g("Mcols")
                            first = False
                    last = (kt5 == n512 - 1)
                    t = P.op("tensor", lambda e, Q_=Q_, K_=K_, j=j, kt5=kt5,
                             idx=idx, last=last: e.matmul(
                        bank(idx),
                        Q_[0:68, 128 * j:128 * j + 128],
                        K_[0:68, 512 * kt5:512 * kt5 + 512],
                        start=True, stop=not last), waits=wl, inc="t")
                    if last:
                        t = P.op("tensor", lambda e, j=j, idx=idx: e.matmul(
                            bank(idx), id_bf[:], m1_sb[:, j, :],
                            start=False, stop=True), inc="t")
                dst = Mcols[:, j:j + 1] if chn == 0 else Mtmp[:, 0:1]
                wl = [("t", t)]
                if chn > 0:
                    wl.append(("v", vloc))
                vloc = P.op("vector", lambda e, dst=dst, nt_=len(tiles): e.reduce_max(
                    dst, bank(0, n=512 * nt_), axis=AX.X), waits=wl, inc="v")
                for bi in range(len(tiles)):
                    bset(bi, "v", vloc)
                if chn > 0:
                    vloc = P.op("vector", lambda e, j=j: e.tensor_tensor(
                        Mcols[:, j:j + 1], Mcols[:, j:j + 1], Mtmp[:, 0:1], ALU.max),
                        waits=[("v", vloc)], inc="v")
                yield
        v_mbf = P.op("vector", lambda e: e.tensor_copy(Mcols_bf[:], Mcols[:]),
                     waits=[("v", vloc)], inc="v")
        P.setg("Mcols", "v", v_mbf)
        s = None
        for J in range(4):
            for jj in range(4):
                j = 4 * J + jj
                wl = []
                if jj == 0:
                    wl += bwait(7) + [("v", v_mbf)]
                t = P.op("tensor", lambda e, j=j, jj=jj: e.transpose(
                    psum_bf[0:1, 7 * 1024 + 128 * jj:7 * 1024 + 128 * (jj + 1)],
                    Mcols_bf[:, j:j + 1], id_bf[:]), waits=wl, inc="t")
            s = P.op("scalar", lambda e, J=J: e.copy(
                Mrow[0:1, 512 * J:512 * J + 512],
                psum_bf[0:1, 7 * 1024:7 * 1024 + 512]),
                waits=[("t", t)] + (P.g("Mrow") if J == 0 else []), inc="s")
            bset(7, "s", s)
            yield
        mv_M[h] = P.dma("sync", lambda sy, Q_=Q_: sy.dma_start(
            out=Q_[64:65, :], in_=Mrow[0:1, :]), waits=[("s", s)],
            inc=f"sl{slot}")
        P.setg("Mrow", f"sl{slot}", mv_M[h])

    def p2_gen(h):
        slot = h % NSLOT
        Q_, K_, V_ = Qh[slot], Kh[slot], Vh[slot]
        first = True
        pend = None
        t = None

        def emit_pv(p):
            par, kt0, exp_s, J = p
            nkt = 8 * J + 8
            tl = None
            for u in range(2):
                kt = kt0 + u
                wl = [("s", exp_s)] if u == 0 else []
                if kt == 0:
                    wl += bwait(6) + [(f"sl{slot}", vg_init[h])]
                tl = P.op("tensor", lambda e, kt=kt, par=par, u=u, nkt=nkt: e.matmul(
                    bank(6, p1=65),
                    V_[:, kt, :],
                    PT[par][:, 512 * u:512 * u + 512],
                    start=(kt == 0), stop=(kt == nkt - 1)), waits=wl, inc="t")
            P.setg(f"PT{par}", "t", tl)
            return tl

        for J in range(4):
            nkt = 8 * J + 8
            for ktp in range(nkt // 2):
                par = ktp % 2
                b0, b1 = (2, 3) if par == 0 else (4, 5)
                for u, bb in ((0, b0), (1, b1)):
                    kt = 2 * ktp + u
                    masked = kt >= 8 * J
                    wl = []
                    if u == 0:
                        wl += bwait(b0, b1)
                        if first:
                            wl.append((f"sl{slot}", mv_M[h]))
                            first = False
                    t = P.op("tensor", lambda e, K_=K_, Q_=Q_, J=J, kt=kt, bb=bb,
                             masked=masked: e.matmul(
                        bank(bb),
                        K_[0:68, 128 * kt:128 * kt + 128],
                        Q_[0:68, 512 * J:512 * J + 512],
                        start=True, stop=not masked), waits=wl, inc="t")
                    if masked:
                        t = P.op("tensor", lambda e, J=J, kt=kt, bb=bb: e.matmul(
                            bank(bb), id_bf[:], m2_sb[:, J, kt - 8 * J, :],
                            start=False, stop=True), inc="t")
                wl = [("t", t)] + P.g(f"PT{par}")
                s = P.op("scalar", lambda e, par=par, b0=b0: e.activation(
                    PT[par][:], psum[0:128, 512 * b0:512 * b0 + 1024], ACT.Exp),
                    waits=wl, inc="s")
                bset(b0, "s", s)
                bset(b1, "s", s)
                _LAG = True
                if _LAG:
                    if pend is not None:
                        emit_pv(pend)
                    pend = (par, 2 * ktp, s, J)
                else:
                    tl = emit_pv((par, 2 * ktp, s, J))
                yield
            if pend is not None:
                tl = emit_pv(pend)
            pend = None
            par_y = J % 2
            wl = [("t", tl)] + P.g(f"ystage{par_y}")
            vv = P.op("vector", lambda e, par_y=par_y: e.tensor_copy(
                ystage[par_y][:], bank(6, p1=64)), waits=wl, inc="v")
            vv2 = P.op("vector", lambda e, J=J: e.tensor_copy(
                sums_stage[64:65, 512 * J:512 * J + 512], bank(6, p0=64, p1=65)),
                waits=(P.g("sums_stage") if J == 0 else []), inc="v")
            bset(6, "v", vv2)
            ymv = P.dma("sync", lambda sy, h=h, J=J, par_y=par_y: sy.dma_start(
                out=y_scr[:, h, 512 * J:512 * J + 512], in_=ystage[par_y][:]),
                waits=[("v", vv)], inc="yw0" if par_y == 0 else "yw1")
            P.setg(f"ystage{par_y}", "yw0" if par_y == 0 else "yw1", ymv)
            yield
        mvv = P.dma("sync", lambda sy, h=h: sy.dma_start(
            out=sums_all[h:h + 1, :], in_=sums_stage[64:65, :]),
            waits=[("v", vv2)], inc="sw")
        P.setg("sums_stage", "sw", mvv)
        P.setg(f"slot{slot}", "t", P.cnt["t"])

    _PIPELINE = True
    load_head(0)
    load_head(1)
    if _PIPELINE:
        g_prev = None
        for h in range(H):
            if h + 2 < H:
                load_head(h + 2)
            g_cur = p1_gen(h)
            while True:
                try:
                    next(g_cur)
                except StopIteration:
                    g_cur = None
                if g_prev is not None:
                    try:
                        next(g_prev)
                    except StopIteration:
                        g_prev = None
                if g_cur is None and g_prev is None:
                    break
            g_prev = p2_gen(h)
        for _ in g_prev:
            pass
    else:
        for h in range(H):
            if h + 2 < H:
                load_head(h + 2)
            for _ in p1_gen(h):
                pass
            for _ in p2_gen(h):
                pass
    mv_B = P.cnt["sw"]
    yw_B = (P.cnt["yw0"], P.cnt["yw1"])

    P.barrier()
    ctxB.close()

    ctxC = contextlib.ExitStack()

    def sbC(name, shape, dtype):
        return ctxC.enter_context(nc.sbuf_tensor(name, list(shape), dtype))

    Wp_sb = sbC("Wp_sb", [64, 16, C], BF16)
    ytile = [sbC(f"ytile{i}", [64, 16, 128], BF16) for i in range(3)]
    rtile = [sbC(f"rtile{i}", [64, 16, 128], F32) for i in range(3)]
    y_n = [sbC(f"y_n{i}", [64, 16, 128], BF16) for i in range(3)]
    bproj_bc = sbC("bproj_bc", [128, C], F32)
    xo_t = [sbC(f"xo_t{i}", [128, C], F32) for i in range(2)]
    xb_t = sbC("xb_t", [128, C], F32)
    zbuf = sbC("zbuf", [128, 4, C], F32)
    zsq = sbC("zsq", [128, C], F32)
    lnst = sbC("lnst", [128, 24], F32)
    x2t = [sbC(f"x2t{i}", [128, C], F32) for i in range(2)]
    x2bf = [sbC(f"x2bf{i}", [128, C], BF16) for i in range(2)]
    fm_stage = sbC("fm_stage", [128, 8, 128], BF16)

    P.dma("gpsimd", lambda g_: g_.dma_start(
        out=Wp_sb[:], in_=W_proj.rearrange("(kc p) m -> p kc m", p=64)), inc="gc")
    gc = P.dma("gpsimd", lambda g_: g_.dma_start(
        out=bproj_bc[:], in_=b_proj[None, :].broadcast_to([128, C])), inc="gc")
    sr_bproj = gc
    recips = sbC("recips", [16, NT], F32)
    v = P.op("vector", lambda e: e.reciprocal(recips[:], sums_all[:]),
             waits=[("sw", mv_B)], inc="v")
    sr_rec = P.dma("sync", lambda sy: sy.dma_start(
        out=vec_scr[0:16, :], in_=recips[:]), waits=[("v", v)], inc="sr")

    def emit_ln_finalize(ln, producer_v):
        vv = P.op("vector", lambda e: e.tensor_scalar_mul(
            ln[:, 0:4], ln[:, 0:4], 1.0 / C), waits=[("v", producer_v)], inc="v")
        vv = P.op("vector", lambda e: e.tensor_scalar_mul(
            ln[:, 4:8], ln[:, 4:8], 1.0 / C), inc="v")
        vv = P.op("vector", lambda e: e.tensor_tensor(
            ln[:, 8:12], ln[:, 0:4], ln[:, 0:4], ALU.mult), waits=[("v", vv)], inc="v")
        vv = P.op("vector", lambda e: e.tensor_tensor(
            ln[:, 4:8], ln[:, 4:8], ln[:, 8:12], ALU.subtract), waits=[("v", vv)], inc="v")
        vv = P.op("vector", lambda e: e.tensor_scalar_add(
            ln[:, 4:8], ln[:, 4:8], 1e-5), waits=[("v", vv)], inc="v")
        vv = P.op("vector", lambda e: e.reciprocal(
            ln[:, 8:12], ln[:, 4:8]), waits=[("v", vv)], inc="v")
        ss = P.op("scalar", lambda e: e.activation(
            ln[:, 12:16], ln[:, 8:12], ACT.Sqrt), waits=[("v", vv)], inc="s")
        vv = P.op("vector", lambda e: e.tensor_tensor(
            ln[:, 16:20], ln[:, 12:16], ln[:, 12:16], ALU.mult),
            waits=[("s", ss)], inc="v")
        vv = P.op("vector", lambda e: e.tensor_tensor(
            ln[:, 16:20], ln[:, 16:20], ln[:, 4:8], ALU.mult), waits=[("v", vv)], inc="v")
        vv = P.op("vector", lambda e: e.tensor_scalar(
            ln[:, 16:20], ln[:, 16:20], -0.5, 1.5, ALU.mult, ALU.add),
            waits=[("v", vv)], inc="v")
        vv = P.op("vector", lambda e: e.tensor_tensor(
            ln[:, 20:24], ln[:, 12:16], ln[:, 16:20], ALU.mult), waits=[("v", vv)], inc="v")
        return vv

    sx = {0: 0, 1: 0}
    sb_c = P.cnt["sb"]
    pending_tail = []
    tail_state = {"q": 0}
    lnsnap = sbC("lnsnap", [128, 8], F32)

    def emit_tail_tile():
        if not pending_tail:
            return
        tg0, fin_v = pending_tail[0]
        q = tail_state["q"]
        tt = 4 * tg0 + q
        par = tt % 2
        vv = P.op("vector", lambda e, q=q, par=par: e.tensor_scalar(
            x2t[par][:], zbuf[:, q, :], lnsnap[:, q:q + 1], lnsnap[:, 4 + q:5 + q],
            ALU.subtract, ALU.mult), waits=[("v", fin_v)] + P.g(f"x2t{par}"), inc="v")
        P.setg(f"zbuf{q}", "v", vv)
        sa2 = P.dma("sync", lambda sy, tt=tt, par=par: sy.dma_start(
            out=x2_scr[128 * tt:128 * tt + 128, :], in_=x2t[par][:]),
            waits=[("v", vv)], inc=f"xs{par}")
        ss0 = P.op("scalar", lambda e, par=par: e.copy(x2bf[par][:], x2t[par][:]),
                   waits=[("v", vv)] + P.g(f"x2bf{par}"), inc="s")
        tl = None
        for c in range(8):
            wl = []
            if c == 0:
                wl += bwait(6) + [("s", ss0)]
            tl = P.op("tensor", lambda e, c=c, par=par: e.transpose(
                psum_bf[0:128, 6 * 1024 + 128 * c:6 * 1024 + 128 * (c + 1)],
                x2bf[par][:, 128 * c:128 * c + 128], id_bf[:]), waits=wl, inc="t")
        P.setg(f"x2bf{par}", "t", tl)
        ss = P.op("scalar", lambda e: e.copy(
            fm_stage[:], psum_bf[0:128, 6 * 1024:6 * 1024 + 1024]),
            waits=[("t", tl)] + P.g("fm_stage"), inc="s")
        bset(6, "s", ss)
        sa2 = P.dma("sync", lambda sy, tt=tt, par=par: sy.dma_start(
            out=x2fm_scr[:, :, 128 * tt:128 * tt + 128].rearrange("c p m -> p c m"),
            in_=fm_stage[:]), waits=[("s", ss)], inc=f"xs{par}")
        P.setg("fm_stage", f"xs{par}", sa2)
        P.setg(f"x2t{par}", f"xs{par}", sa2)
        if q == 3:
            pending_tail.pop(0)
            tail_state["q"] = 0
        else:
            tail_state["q"] = q + 1

    for tg in range(4):
        for q in range(4):
            tt = 4 * tg + q
            emit_tail_tile()
            par3 = tt % 3
            par = tt % 2
            sxs = "sx0" if par == 0 else "sx1"
            sb_c = P.dma("gpsimd", lambda g_, tt=tt, par3=par3: g_.dma_start(
                out=ytile[par3][:], in_=y_scr[:, :, 128 * tt:128 * tt + 128]),
                waits=P.g(f"ytile{par3}")
                      + ([("yw0", yw_B[0]), ("yw1", yw_B[1])] if tt < 3 else []),
                inc=f"yt{par3}")
            sr = P.dma("gpsimd", lambda g_, tt=tt, par3=par3: g_.dma_start(
                out=rtile[par3][:],
                in_=vec_scr[None, 0:16, 128 * tt:128 * tt + 128]
                .broadcast_to([64, 16, 128])),
                waits=[("sr", sr_rec)] + P.g(f"rtile{par3}"), inc=f"rt{par3}")
            v_yn = P.op("vector", lambda e, par3=par3: e.tensor_tensor(
                y_n[par3][:], ytile[par3][:], rtile[par3][:], ALU.mult),
                waits=[(f"yt{par3}", sb_c), (f"rt{par3}", sr)] + P.g(f"y_n{par3}"),
                inc="v")
            P.setg(f"ytile{par3}", "v", v_yn)
            P.setg(f"rtile{par3}", "v", v_yn)
            sx[par] = P.dma("gpsimd", lambda g_, tt=tt, par=par: g_.dma_start(
                out=xo_t[par][:], in_=x_own[128 * tt:128 * tt + 128, :]),
                waits=P.g(f"xo{par}"), inc=f"xo{par}")
            v_xb = P.op("vector", lambda e, par=par: e.tensor_tensor(
                xb_t[:], xo_t[par][:], bproj_bc[:], ALU.add),
                waits=[(f"xo{par}", sx[par]), ("gc", sr_bproj)] + P.g("xb"), inc="v")
            P.setg(f"xo{par}", "v", v_xb)
            for nh in range(2):
                bk_i = 2 * (tt % 3) + nh
                for kc in range(16):
                    wl = []
                    if kc == 0:
                        wl += bwait(bk_i) + [("v", v_yn)]
                        if tt == 0 and nh == 0:
                            wl += [("gc", gc)]
                    t = P.op("tensor", lambda e, par3=par3, nh=nh, kc=kc, bk_i=bk_i: e.matmul(
                        bank(bk_i),
                        y_n[par3][:, kc, :],
                        Wp_sb[0:64, kc, 512 * nh:512 * nh + 512],
                        start=(kc == 0), stop=(kc == 15)), waits=wl, inc="t")
                v = P.op("vector", lambda e, q=q, nh=nh, bk_i=bk_i: e.tensor_tensor(
                    zbuf[:, q, 512 * nh:512 * nh + 512], bank(bk_i),
                    xb_t[:, 512 * nh:512 * nh + 512], ALU.add),
                    waits=[("t", t)] + ([("v", v_xb)] if nh == 0 else []) +
                          (P.g(f"zbuf{q}") if nh == 0 else []), inc="v")
                bset(bk_i, "v", v)
            P.setg("xb", "v", v)
            P.setg(f"y_n{par3}", "t", t)
            v = P.op("vector", lambda e, q=q: e.reduce_sum(
                lnst[:, q:q + 1], zbuf[:, q, :], axis=AX.X), waits=[("v", v)], inc="v")
            v = P.op("vector", lambda e, q=q: e.tensor_tensor_reduce(
                out=zsq[:], in0=zbuf[:, q, :], in1=zbuf[:, q, :], scale=1.0,
                scalar=0.0, op0=ALU.mult, op1=ALU.add,
                accum_out=lnst[:, 4 + q:5 + q]), waits=[("v", v)], inc="v")
        fin_v = emit_ln_finalize(lnst, v)
        fin_v = P.op("vector", lambda e: e.tensor_copy(lnsnap[:, 0:4], lnst[:, 0:4]),
                     waits=[("v", fin_v)], inc="v")
        fin_v = P.op("vector", lambda e: e.tensor_copy(lnsnap[:, 4:8], lnst[:, 20:24]),
                     waits=[("v", fin_v)], inc="v")
        pending_tail.append((tg, fin_v))
    while pending_tail:
        emit_tail_tile()
    C_store_waits = [("xs0", P.cnt["xs0"]), ("xs1", P.cnt["xs1"])]

    P.barrier()
    ctxC.close()
    ctxBC.close()

    ctxD = contextlib.ExitStack()

    def sbD(name, shape, dtype):
        return ctxD.enter_context(nc.sbuf_tensor(name, list(shape), dtype))

    a_buf = sbD("a_buf", [128, 32, 1024], BF16)
    b_buf = sbD("b_buf", [128, 32, 1024], BF16)
    x2fm_p = sbD("x2fm_p", [128, 8, 1024], BF16)
    w1c = [sbD(f"w1c{i}", [128, 8, 128], BF16) for i in range(2)]
    w2c = [sbD(f"w2c{i}", [128, 8, 128], BF16) for i in range(2)]
    hc = [sbD(f"hc{i}", [128, 16, 128], BF16) for i in range(2)]
    w3c = [sbD(f"w3c{i}", [128, C], BF16) for i in range(2)]
    w3s_bc = sbD("w3s_bc", [128, C], F32)
    silu_t = sbD("silu_t", [128, 1024], F32)
    x2r = [sbD(f"x2r{i}", [128, C], F32) for i in range(2)]
    zbuf2 = sbD("zbuf2", [128, 4, C], F32)
    lnst2 = sbD("lnst2", [128, 24], F32)
    outb = [sbD("outb0", [128, C], F32)]
    zsq2 = silu_t

    sr_w3s = P.dma("sync", lambda sy: sy.dma_start(
        out=w3s_bc[:], in_=w3s[None, :].broadcast_to([128, C])),
        waits=C_store_waits, inc="sr")

    wa = {0: 0, 1: 0}
    wh = {0: 0, 1: 0}
    wc = {0: 0, 1: 0}

    for p in range(2):
        gx = P.dma("sync", lambda sy, p=p: sy.dma_start(
            out=x2fm_p[:], in_=x2fm_scr[:, :, 1024 * p:1024 * p + 1024]
            .rearrange("c p2 m -> p2 c m")),
            waits=C_store_waits + P.g("x2fm_p"), inc="gx")
        glu_v = {}
        evac_s = {}
        for o in range(32):
            par = o % 2
            was = "wa0" if par == 0 else "wa1"
            P.dma("gpsimd", lambda g_, o=o, par=par: g_.dma_start(
                out=w1c[par][:], in_=w1T[:, 128 * o:128 * o + 128]
                .rearrange("(c q2) m -> q2 c m", q2=128)),
                waits=P.g(f"w12c{par}"), inc=was)
            wa[par] = P.dma("gpsimd", lambda g_, o=o, par=par: g_.dma_start(
                out=w2c[par][:], in_=w2T[:, 128 * o:128 * o + 128]
                .rearrange("(c q2) m -> q2 c m", q2=128)), inc=was)
            for ts_ in range(2):
                t_of = {}
                for which, wbuf, bset_i in (("w1", w1c[par], 0), ("w2", w2c[par], 2)):
                    bk_i = bset_i + ts_
                    for c in range(8):
                        wl = []
                        if c == 0:
                            wl += bwait(bk_i) + [(was, wa[par])]
                            if o == 0 and ts_ == 0 and which == "w1":
                                wl.append(("gx", gx))
                        t = P.op("tensor", lambda e, wbuf=wbuf, c=c, ts_=ts_,
                                 bk_i=bk_i: e.matmul(
                            bank(bk_i),
                            wbuf[:, c, :],
                            x2fm_p[:, c, 512 * ts_:512 * ts_ + 512],
                            start=(c == 0), stop=(c == 7)), waits=wl, inc="t")
                    t_of[which] = t
                s = P.op("scalar", lambda e, o=o, ts_=ts_: e.activation(
                    silu_t[:, 512 * ts_:512 * ts_ + 512], bank(ts_), ACT.Silu,
                    scale=scales[:, 0, o:o + 1]),
                    waits=[("t", t_of["w1"])] + P.g(f"silu{ts_}")
                          + (P.g("silu_scr") if o == 0 else []), inc="s")
                bset(ts_, "s", s)
                v = P.op("vector", lambda e, o=o, ts_=ts_: e.scalar_tensor_tensor(
                    a_buf[:, o, 512 * ts_:512 * ts_ + 512], bank(2 + ts_),
                    scales[:, 1, o:o + 1], silu_t[:, 512 * ts_:512 * ts_ + 512],
                    ALU.mult, ALU.mult),
                    waits=[("t", t_of["w2"]), ("s", s)] + (P.g("w3ab") if o == 0 and ts_ == 0 else []),
                    inc="v")
                bset(2 + ts_, "v", v)
                P.setg(f"silu{ts_}", "v", v)
            glu_v_local = v
            P.setg(f"w12c{par}", "t", t_of["w2"])
            glu_v[o] = glu_v_local
        w12_done_v = v
        prev_s = None
        for li, (hT_, src, dstb) in enumerate(((h1T, a_buf, b_buf), (h2T, b_buf, a_buf))):
            scol = 2 + li
            for o in range(32):
                for half in range(2):
                    hidx = 2 * o + half
                    par = hidx % 2
                    whs = "wh0" if par == 0 else "wh1"
                    wh[par] = P.dma("gpsimd", lambda g_, hT_=hT_, o=o, par=par,
                                    half=half: g_.dma_start(
                        out=hc[par][:],
                        in_=hT_[2048 * half:2048 * half + 2048, 128 * o:128 * o + 128]
                        .rearrange("(c q2) m -> q2 c m", q2=128)),
                        waits=P.g(f"hc{par}"), inc=whs)
                    for ts_ in range(2):
                        bk_i = 4 + ts_
                        for c in range(16):
                            cglob = 16 * half + c
                            wl = []
                            if c == 0:
                                wl.append((whs, wh[par]))
                                if half == 0:
                                    wl += bwait(bk_i)
                            if ts_ == 0 and o == 0:
                                if li == 0:
                                    wl.append(("v", glu_v[cglob]))
                                else:
                                    wl.append(("s", evac_s[(0, cglob)]))
                            t = P.op("tensor", lambda e, src=src, par=par, c=c, ts_=ts_,
                                     half=half, bk_i=bk_i: e.matmul(
                                bank(bk_i),
                                hc[par][:, c, :],
                                src[:, 16 * half + c, 512 * ts_:512 * ts_ + 512],
                                start=(half == 0 and c == 0),
                                stop=(half == 1 and c == 15)), waits=wl, inc="t")
                    P.setg(f"hc{par}", "t", t)
                for ts_ in range(2):
                    bk_i = 4 + ts_
                    s = P.op("scalar", lambda e, dstb=dstb, o=o, ts_=ts_, bk_i=bk_i,
                             scol=scol: e.activation(
                        dstb[:, o, 512 * ts_:512 * ts_ + 512], bank(bk_i), ACT.Gelu,
                        scale=scales[:, scol, o:o + 1]), waits=[("t", t)], inc="s")
                    bset(bk_i, "s", s)
                evac_s[(li, o)] = s
            prev_s = s
        h2_done_s = s
        if p == 0:
            v = P.op("vector", lambda e: e.memset(lnst2[:], 0.5), inc="v")
        for tg_ in range(2):
            sx_of_q = {}
            for q in range(2):
                tt = 8 * p + 4 * tg_ + q
                par = q % 2
                sxs = "sx0" if par == 0 else "sx1"
                sx_of_q[q] = P.dma("sync", lambda sy, tt=tt, par=par: sy.dma_start(
                    out=x2r[par][:], in_=x2_scr[128 * tt:128 * tt + 128, :]),
                    waits=P.g(f"x2r{par}"), inc=sxs)
            for c in range(32):
                par = c % 2
                wcs = "wc0" if par == 0 else "wc1"
                wc[par] = P.dma("gpsimd", lambda g_, c=c, par=par: g_.dma_start(
                    out=w3c[par][:], in_=w3T[128 * c:128 * c + 128, :]),
                    waits=P.g(f"w3c{par}"), inc=wcs)
                for q in range(4):
                    for nh in range(2):
                        bk_i = 2 * q + nh
                        wl = []
                        if q == 0 and nh == 0:
                            wl.append((wcs, wc[par]))
                            if tg_ == 0:
                                wl.append(("s", evac_s[(1, c)]))
                        if c == 0:
                            wl += bwait(bk_i)
                        t = P.op("tensor", lambda e, c=c, q=q, nh=nh, tg_=tg_,
                                 par=par, bk_i=bk_i: e.matmul(
                            bank(bk_i),
                            a_buf[:, c, 512 * tg_ + 128 * q:512 * tg_ + 128 * q + 128],
                            w3c[par][:, 512 * nh:512 * nh + 512],
                            start=(c == 0), stop=(c == 31)), waits=wl, inc="t")
                P.setg(f"w3c{par}", "t", t)
            t_w3 = t
            for q in range(4):
                par = q % 2
                sxs = "sx0" if par == 0 else "sx1"
                for nh in range(2):
                    bk_i = 2 * q + nh
                    wl = []
                    if nh == 0:
                        wl = [("t", t_w3), (sxs, sx_of_q[q])] + P.g(f"zb2_{q}")
                    v = P.op("vector", lambda e, q=q, nh=nh, bk_i=bk_i, par=par: e.scalar_tensor_tensor(
                        zbuf2[:, q, 512 * nh:512 * nh + 512], bank(bk_i),
                        1.0, w3s_bc[:, 512 * nh:512 * nh + 512],
                        ALU.mult, ALU.mult), waits=wl + ([("sr", sr_w3s)] if q == 0 and nh == 0 else []), inc="v")
                    v = P.op("vector", lambda e, q=q, nh=nh, par=par: e.tensor_tensor(
                        zbuf2[:, q, 512 * nh:512 * nh + 512],
                        zbuf2[:, q, 512 * nh:512 * nh + 512],
                        x2r[par][:, 512 * nh:512 * nh + 512], ALU.add),
                        waits=[("v", v)], inc="v")
                    bset(bk_i, "v", v)
                P.setg(f"x2r{par}", "v", v)
                if q + 2 <= 3:
                    q2 = q + 2
                    tt2 = 8 * p + 4 * tg_ + q2
                    sx_of_q[q2] = P.dma("sync", lambda sy, tt2=tt2, par=par: sy.dma_start(
                        out=x2r[par][:], in_=x2_scr[128 * tt2:128 * tt2 + 128, :]),
                        waits=[("v", v)], inc=sxs)
                v = P.op("vector", lambda e, q=q: e.reduce_sum(
                    lnst2[:, q:q + 1], zbuf2[:, q, :], axis=AX.X), waits=[("v", v)], inc="v")
                v = P.op("vector", lambda e, q=q: e.tensor_tensor_reduce(
                    out=zsq2[:], in0=zbuf2[:, q, :], in1=zbuf2[:, q, :], scale=1.0,
                    scalar=0.0, op0=ALU.mult, op1=ALU.add,
                    accum_out=lnst2[:, 4 + q:5 + q]), waits=[("v", v)], inc="v")
                P.setg("silu_scr", "v", v)
            v = emit_ln_finalize(lnst2, v)
            for q in range(4):
                tt = 8 * p + 4 * tg_ + q
                v = P.op("vector", lambda e, q=q: e.tensor_scalar(
                    outb[0][:], zbuf2[:, q, :], lnst2[:, q:q + 1],
                    lnst2[:, 20 + q:21 + q], ALU.subtract, ALU.mult),
                    waits=[("v", v)] + P.g("outb0"), inc="v")
                P.setg(f"zb2_{q}", "v", v)
                oc_ = P.dma("sync", lambda sy, tt=tt: sy.dma_start(
                    out=out_h[128 * tt:128 * tt + 128, :], in_=outb[0][:]),
                    waits=[("v", v)], inc="out")
                P.setg("outb0", "out", oc_)
        P.setg("x2fm_p", "t", P.cnt["t"])

    P.wait("sync", [("out", P.cnt["out"])])

    with nc.Block() as block:
        with contextlib.ExitStack() as es:
            sems = {sname: es.enter_context(nc.semaphore(sname)) for sname in SEMS}
            P.emit(block, sems)
    ctxD.close()
    ctx.close()
    mybir.codegen_inst_isa_subclasses(nc)
    return nc


_slope0 = 2.0 ** (-2.0 ** (-(math.log2(H) - 3)))
_SLOPES = [_slope0 * _slope0 ** i for i in range(H)]


def _host_consts(half):
    subs = SUBS[half]
    m1 = np.zeros((16, 128, 512), dtype=np.float32)
    for j in range(16):
        g = subs[j // 2]
        qp = 256 * g + 128 * (j % 2)
        kbase = 512 * (KTJ[j] - 1)
        kk = kbase + np.arange(512)[None, :]
        qq = qp + np.arange(128)[:, None]
        m1[j] = np.where(kk <= qq, 0.0, NEG)
    m2 = np.zeros((4, 8, 128, 512), dtype=np.float32)
    for J in range(4):
        qglob = np.empty(512, dtype=np.int64)
        for hf in range(2):
            g = subs[2 * J + hf]
            qglob[256 * hf:256 * hf + 256] = 256 * g + np.arange(256)
        for ti in range(8):
            kt = 8 * J + ti
            kk = 128 * kt + np.arange(128)[:, None]
            m2[J, ti] = np.where(kk <= qglob[None, :], 0.0, NEG)
    return m1.astype(bf16), m2.astype(bf16)


def _bias_rows():
    br = np.zeros((H, 4, T), dtype=bf16)
    br[:, 0, :] = bf16(-1.0)
    k = np.arange(T, dtype=np.float64)
    for h in range(H):
        b = _SLOPES[h] * k
        b1 = b.astype(np.float32).astype(bf16)
        r = b - b1.astype(np.float64)
        b2 = r.astype(np.float32).astype(bf16)
        r = r - b2.astype(np.float64)
        b3 = r.astype(np.float32).astype(bf16)
        br[h, 1], br[h, 2], br[h, 3] = b1, b2, b3
    return br


_NC = None


def kernel(L0_attn_x, L0_attn_W_attn, L0_attn_b_attn, L0_attn_W_proj,
           L0_attn_b_proj, L0_attn_temperature, L0_ffn_w1_w, L0_ffn_w1_s,
           L0_ffn_w2_w, L0_ffn_w2_s, L0_ffn_h1_w, L0_ffn_h1_s, L0_ffn_h2_w,
           L0_ffn_h2_s, L0_ffn_w3_w, L0_ffn_w3_s):
    global _NC
    if _NC is None:
        _NC = _build()
    nc = _NC

    x = np.asarray(L0_attn_x, dtype=np.float32)
    shared = {
        "W_attn": np.asarray(L0_attn_W_attn, np.float32),
        "b_attn": np.asarray(L0_attn_b_attn, np.float32),
        "W_proj": np.asarray(L0_attn_W_proj, np.float32),
        "b_proj": np.asarray(L0_attn_b_proj, np.float32),
        "temperature": np.asarray(L0_attn_temperature, np.float32),
        "w1T": np.ascontiguousarray(np.asarray(L0_ffn_w1_w, np.float32).T),
        "w2T": np.ascontiguousarray(np.asarray(L0_ffn_w2_w, np.float32).T),
        "h1T": np.ascontiguousarray(np.asarray(L0_ffn_h1_w, np.float32).T),
        "h2T": np.ascontiguousarray(np.asarray(L0_ffn_h2_w, np.float32).T),
        "w3T": np.ascontiguousarray(np.asarray(L0_ffn_w3_w, np.float32).T),
        "w3s": np.asarray(L0_ffn_w3_s, np.float32),
        "scales_in": np.ascontiguousarray(np.stack(
            [np.asarray(a, np.float32).reshape(32, 128).T
             for a in (L0_ffn_w1_s, L0_ffn_w2_s, L0_ffn_h1_s, L0_ffn_h2_s)], axis=1)),
        "bqk_in": np.ascontiguousarray(np.stack(
            [np.asarray(L0_attn_b_attn[0:1024], np.float32).reshape(8, 128).T,
             np.asarray(L0_attn_b_attn[1024:2048], np.float32).reshape(8, 128).T],
            axis=1)),
        "ident": np.eye(128, dtype=bf16),
        "biasrows": _bias_rows(),
        "qinit": np.concatenate([np.zeros((1, NT), bf16), np.ones((3, NT), bf16)]),
    }
    consts_h = [_host_consts(0), _host_consts(1)]

    in_maps = []
    for c in range(8):
        b, half = c // 2, c % 2
        subs = SUBS[half]
        xo = np.concatenate([x[b, 256 * g:256 * g + 256, :] for g in subs], axis=0)
        m1, m2 = consts_h[half]
        im = dict(shared)
        im["x_own"] = np.ascontiguousarray(xo)
        im["xT_own"] = np.ascontiguousarray(xo.T)
        im["xT_b"] = np.ascontiguousarray(x[b].T)
        im["masks1"] = m1
        im["masks2"] = m2
        in_maps.append(im)

    res = run_bass_kernel_spmd(nc, in_maps, core_ids=list(range(8)))

    out = np.empty((B, T, C), dtype=np.float32)
    for c in range(8):
        b, half = c // 2, c % 2
        subs = SUBS[half]
        co = res.results[c]["out"]
        for idx, g in enumerate(subs):
            out[b, 256 * g:256 * g + 256, :] = co[256 * idx:256 * idx + 256, :]
    return out


# version 26 (final)
# speedup vs baseline: 1.0127x; 1.0127x over previous
import math
import contextlib
import numpy as np
import ml_dtypes
import concourse.bass as bass
import concourse.mybir as mybir
from concourse.bass_utils import run_bass_kernel_spmd

F32 = mybir.dt.float32
BF16 = mybir.dt.bfloat16
AX = mybir.AxisListType
ALU = mybir.AluOpType
ACT = mybir.ActivationFunctionType
bf16 = ml_dtypes.bfloat16

B, T, C, H, D, HID = 4, 4096, 1024, 16, 64, 4096
NT = 2048
SUBS = ([0, 3, 4, 7, 8, 11, 12, 15], [1, 2, 5, 6, 9, 10, 13, 14])
NEG = -1.0e30
KTJ = [j // 2 + 1 for j in range(16)]

ENGINES = ("tensor", "vector", "scalar", "gpsimd", "sync")
SEMS = ["t", "v", "s", "bar", "ga", "sa", "sb", "mv", "mw", "yw0", "yw1", "sw",
        "gc", "sr", "gx", "wa0", "wa1", "wh0", "wh1", "wc0", "wc1", "sx0", "sx1", "out",
        "sl0", "sl1", "sl2", "sl3", "sl4", "yt0", "yt1", "yt2", "rt0", "rt1", "rt2",
        "xo0", "xo1", "xs0", "xs1", "bv", "ka0", "ka1", "qa0", "qa1", "va0", "va1", "gp0", "gp1"]

_DEV_TAPS = ()


class Prog:

    def __init__(self):
        self.steps = {e: [] for e in ENGINES}
        self.cnt = {sname: 0 for sname in SEMS}
        self.gate = {}

    def op(self, eng, fn, waits=(), inc=None, n=1):
        post = None
        if inc is not None:
            self.cnt[inc] += n
            post = self.cnt[inc]
        self.steps[eng].append((fn, list(waits), inc, n))
        return post

    def dma(self, eng, fn, waits=(), inc=None):
        return self.op(eng, fn, waits, inc, n=16)

    def wait(self, eng, waits):
        self.steps[eng].append((None, list(waits), None, 0))

    ENG_SEMS = ("t", "v", "s")

    def g(self, *names):
        out = []
        for nm in names:
            gt = self.gate.get(nm)
            if gt is None:
                continue
            sem, val = gt
            if sem in self.ENG_SEMS:
                out.append((sem, val))
            else:
                out.append((sem, self.cnt[sem]))
        return out

    def setg(self, nm, sem, val):
        self.gate[nm] = (sem, val)

    def barrier(self):
        base = self.cnt["bar"]
        for e in ENGINES:
            if e == "sync":
                self.op(e, lambda eng: eng.nop(), inc="bar")
            else:
                self.op(e, lambda eng: eng.drain(), inc="bar")
        for e in ENGINES:
            self.wait(e, [("bar", base + len(ENGINES))])

    def emit(self, block, sems):
        def mk(ename):
            def body(eng):
                for fn, waits, inc, n in self.steps[ename]:
                    for (sname, val) in waits:
                        eng.wait_ge(sems[sname], val)
                    if fn is None:
                        continue
                    ins = fn(eng)
                    if inc is not None:
                        ins.then_inc(sems[inc], n)
            return body
        for ename in ENGINES:
            if self.steps[ename]:
                getattr(block, ename)(mk(ename))


def _build():
    nc = bass.Bass("TRN2", target_bir_lowering=False)
    P = Prog()

    def inp(name, shape, dtype=F32):
        return nc.dram_tensor(name, list(shape), dtype, kind="ExternalInput")

    def scratch(name, shape, dtype):
        kind = "ExternalOutput" if name in _DEV_TAPS else "Internal"
        return nc.dram_tensor(name, list(shape), dtype, kind=kind)

    xT_b = inp("xT_b", [C, T])
    xT_own = inp("xT_own", [C, NT])
    x_own = inp("x_own", [NT, C])
    W_attn = inp("W_attn", [C, 3 * C])
    b_attn = inp("b_attn", [3 * C])
    W_proj = inp("W_proj", [C, C])
    b_proj = inp("b_proj", [C])
    temp = inp("temperature", [1])
    w1T = inp("w1T", [C, HID])
    w2T = inp("w2T", [C, HID])
    h1T = inp("h1T", [HID, HID])
    h2T = inp("h2T", [HID, HID])
    w3T = inp("w3T", [HID, C]); w3s = inp("w3s", [C])
    scales_in = inp("scales_in", [128, 4, 32])
    bqk_in = inp("bqk_in", [128, 2, 8])
    ident = inp("ident", [128, 128], BF16)
    masks1 = inp("masks1", [16, 128, 512], BF16)
    masks2 = inp("masks2", [4, 8, 128, 512], BF16)
    biasrows = inp("biasrows", [H, 4, T], BF16)
    qinit = inp("qinit", [4, NT], BF16)

    out_h = nc.dram_tensor("out", [NT, C], F32, kind="ExternalOutput")

    K_scr = scratch("K_scr", [H, 64, T], BF16)
    Q_scr = scratch("Q_scr", [H, 64, NT], BF16)
    V_scr = scratch("V_scr", [T, H * 65], BF16)
    x2_scr = scratch("x2_scr", [NT, C], F32)
    x2fm_scr = scratch("x2fm_scr", [8, 128, NT], BF16)
    vec_scr = scratch("vec_scr", [17, NT], F32)
    y_scr = scratch("y_scr", [64, H, NT], BF16)

    ctx = contextlib.ExitStack()

    def sbP(name, shape, dtype):
        return ctx.enter_context(nc.sbuf_tensor(name, list(shape), dtype))

    id_bf = sbP("id_bf", [128, 128], BF16)
    sbc = sbP("sbc", [128, 1], F32)
    bq_f = sbP("bq_f", [128, 8], F32)
    bq_s = sbP("bq_s", [128, 8], F32)
    bk_sb = sbP("bk_sb", [128, 8], F32)
    tstat = sbP("tstat", [1, 4], F32)
    srow_t = sbP("srow_t", [1, 128], F32)
    scales = sbP("scales", [128, 4, 32], F32)
    psum_h = ctx.enter_context(nc.psum_tensor("ps", [128, 4096], F32))
    psum = psum_h
    psum_bf = psum_h.bitcast(BF16)

    def bank(i, p0=0, p1=128, off=0, n=512):
        return psum[p0:p1, 512 * i + off:512 * i + off + n]

    def bwait(*banks):
        return P.g(*[f"bank{i}" for i in banks])

    def bset(i, sem, val):
        P.setg(f"bank{i}", sem, val)

    ctxA = contextlib.ExitStack()

    def sbA(name, shape, dtype):
        return ctxA.enter_context(nc.sbuf_tensor(name, list(shape), dtype))

    x_fm = sbA("x_fm", [128, 8, T], BF16)
    xo_fm = sbA("xo_fm", [128, 8, NT], BF16)
    Wa_sb = sbA("Wa_sb", [128, 8, 3 * C], BF16)
    Kstag = [sbA(f"Kstag{i}", [128, T], BF16) for i in range(2)]
    Qstag = [sbA(f"Qstag{i}", [128, NT], BF16) for i in range(2)]
    Vstag = [sbA(f"Vstag{i}", [128, H * 65], BF16) for i in range(2)]
    bv_bc = sbA("bv_bc", [128, C], F32)

    def emit_chunk_loads(c, wl=()):
        P.dma("gpsimd", lambda g_, c=c: g_.dma_start(
            out=x_fm[:, c, :],
            in_=xT_b[128 * c:128 * c + 128, :]), waits=wl, inc=f"gp{c % 2}")
        return P.dma("gpsimd", lambda g_, c=c: g_.dma_start(
            out=Wa_sb[:, c, :],
            in_=W_attn[128 * c:128 * c + 128, :]), inc=f"gp{c % 2}")

    gp_val = {0: emit_chunk_loads(0), 1: emit_chunk_loads(1)}
    xo_loads = 0
    for c in range(8):
        xo_loads = P.dma("gpsimd", lambda g_, c=c: g_.dma_start(
            out=xo_fm[:, c, :],
            in_=xT_own[128 * c:128 * c + 128, :]), inc="mv")
    P.dma("sync", lambda sy: sy.dma_start(out=scales[:], in_=scales_in[:]), inc="sa")
    P.dma("sync", lambda sy: sy.dma_start(out=tstat[0:1, 0:1], in_=temp[None, :]), inc="sa")
    P.dma("sync", lambda sy: sy.dma_start(out=bq_f[:], in_=bqk_in[:, 0, :]), inc="sa")
    P.dma("sync", lambda sy: sy.dma_start(out=bk_sb[:], in_=bqk_in[:, 1, :]), inc="sa")
    sa = P.dma("sync", lambda sy: sy.dma_start(out=id_bf[:], in_=ident[:]), inc="sa")
    v = P.op("vector", lambda e: e.tensor_scalar_mul(tstat[0:1, 1:2], tstat[0:1, 0:1], 8.0),
             waits=[("sa", sa)], inc="v")
    v = P.op("vector", lambda e: e.reciprocal(tstat[0:1, 2:3], tstat[0:1, 1:2]),
             waits=[("v", v)], inc="v")
    v = P.op("vector", lambda e: e.memset(srow_t[:], 1.0), waits=[("v", v)], inc="v")
    v = P.op("vector", lambda e: e.tensor_scalar_mul(
        srow_t[:], srow_t[:], tstat[0:1, 2:3]), waits=[("v", v)], inc="v")
    sa = P.dma("sync", lambda sy: sy.dma_start(
        out=vec_scr[16:17, 0:128], in_=srow_t[:]), waits=[("v", v)], inc="sa")
    sa_sbc = P.dma("sync", lambda sy: sy.dma_start(
        out=sbc[:], in_=vec_scr[16:17, 0:128].rearrange("a (p b) -> (a p) b", p=128)),
        waits=[("sa", sa)], inc="sa")
    v_bqs = P.op("vector", lambda e: e.tensor_scalar_mul(bq_s[:], bq_f[:], sbc[:, 0:1]),
                 waits=[("sa", sa_sbc)], inc="v")
    sa_bv = P.dma("sync", lambda sy: sy.dma_start(
        out=bv_bc[:], in_=b_attn[None, 2 * C:3 * C].broadcast_to([128, C])), inc="bv")

    t = P.cnt["t"]; s = P.cnt["s"]

    for kind, n_j, stag, W_off, x_src, scr in (
            ("K", 8, Kstag, C, x_fm, K_scr), ("Q", 4, Qstag, 0, xo_fm, Q_scr)):
        for oc in range(8):
            stg = stag[oc % 2]
            stg_nm = f"{kind}stag{oc % 2}"
            st_sem = ("ka" if kind == "K" else "qa") + str(oc % 2)
            for j in range(n_j):
                bk_i = (oc * n_j + j) % 2
                for c in range(8):
                    wl = []
                    if kind == "K" and oc == 0 and j == 0:
                        wl.append((f"gp{c % 2}", gp_val[c]))
                        if c == 0:
                            wl.append(("sa", sa_sbc))
                    if c == 0:
                        if kind == "Q" and oc == 0 and j == 0:
                            wl.append(("mv", xo_loads))
                        wl += bwait(bk_i)
                    t = P.op("tensor", lambda e, oc=oc, j=j, c=c, bk_i=bk_i,
                             W_off=W_off, x_src=x_src: e.matmul(
                        bank(bk_i),
                        Wa_sb[:, c, W_off + 128 * oc:W_off + 128 * oc + 128],
                        x_src[:, c, 512 * j:512 * j + 512],
                        start=(c == 0), stop=(c == 7)), waits=wl, inc="t")
                    if kind == "K" and oc == 0 and j == 0 and c + 2 < 8:
                        gp_val[c + 2] = emit_chunk_loads(c + 2, wl=[("t", t)])
                wl = [("t", t)]
                if j == 0:
                    wl += P.g(stg_nm)
                if kind == "Q":
                    wl.append(("v", v_bqs))
                    s = P.op("scalar", lambda e, oc=oc, j=j, bk_i=bk_i, stg=stg: e.activation(
                        stg[:, 512 * j:512 * j + 512], bank(bk_i), ACT.Identity,
                        bias=bq_s[:, oc:oc + 1], scale=sbc[:, 0:1]), waits=wl, inc="s")
                else:
                    s = P.op("scalar", lambda e, oc=oc, j=j, bk_i=bk_i, stg=stg: e.activation(
                        stg[:, 512 * j:512 * j + 512], bank(bk_i), ACT.Identity,
                        bias=bk_sb[:, oc:oc + 1], scale=1.0), waits=wl, inc="s")
                bset(bk_i, "s", s)
            for hh in range(2):
                sa2 = P.dma("sync", lambda sy, oc=oc, hh=hh, stg=stg, scr=scr: sy.dma_start(
                    out=scr[2 * oc + hh], in_=stg[64 * hh:64 * hh + 64, :]),
                    waits=[("s", s)], inc=st_sem)
            P.setg(stg_nm, st_sem, sa2)

    vst_init = []
    for i in range(2):
        vst_init.append(P.op("vector", lambda e, i=i: e.memset(
            Vstag[i].rearrange("p (h c) -> p h c", c=65)[:, :, 64:65], 1.0), inc="v"))
    vst_drain = P.op("vector", lambda e: e.drain(), inc="v")
    for jt in range(32):
        stg = Vstag[jt % 2]
        stg_nm = f"Vstag{jt % 2}"
        for nh in range(2):
            bk_i = 2 + nh
            for c in range(8):
                wl = bwait(bk_i) if c == 0 else []
                t = P.op("tensor", lambda e, jt=jt, nh=nh, c=c, bk_i=bk_i: e.matmul(
                    bank(bk_i),
                    x_fm[:, c, 128 * jt:128 * jt + 128],
                    Wa_sb[:, c, 2 * C + 512 * nh:2 * C + 512 * nh + 512],
                    start=(c == 0), stop=(c == 7)), waits=wl, inc="t")
            wl = [("t", t), ("bv", sa_bv)]
            if nh == 0:
                wl += P.g(stg_nm)
            v = P.op("vector", lambda e, nh=nh, bk_i=bk_i, stg=stg: e.tensor_tensor(
                stg.rearrange("p (h c) -> p h c", c=65)[:, 8 * nh:8 * nh + 8, 0:64],
                bank(bk_i).rearrange("p (h c) -> p h c", c=64),
                bv_bc[:, 512 * nh:512 * nh + 512], ALU.add),
                waits=wl + ([("v", vst_drain)] if jt == 0 and nh == 0 else []), inc="v")
            bset(bk_i, "v", v)
        sa2 = P.dma("sync", lambda sy, jt=jt, stg=stg: sy.dma_start(
            out=V_scr[128 * jt:128 * jt + 128, :], in_=stg[:]),
            waits=[("v", v)], inc="va" + str(jt % 2))
        P.setg(stg_nm, "va" + str(jt % 2), sa2)
    A_store_waits = [("ka0", P.cnt["ka0"]), ("ka1", P.cnt["ka1"]),
                     ("qa0", P.cnt["qa0"]), ("qa1", P.cnt["qa1"]),
                     ("va0", P.cnt["va0"]), ("va1", P.cnt["va1"])]

    P.barrier()
    ctxA.close()

    ctxBC = contextlib.ExitStack()
    sums_all = ctxBC.enter_context(nc.sbuf_tensor("sums_all", [16, NT], F32))

    ctxB = contextlib.ExitStack()

    def sbB(name, shape, dtype):
        return ctxB.enter_context(nc.sbuf_tensor(name, list(shape), dtype))

    NSLOT = 5
    Kh = [sbB(f"Kh{i}", [68, T], BF16) for i in range(NSLOT)]
    Qh = [sbB(f"Qh{i}", [68, NT], BF16) for i in range(NSLOT)]
    Vh = [sbB(f"Vh{i}", [128, 32, 65], BF16) for i in range(NSLOT)]
    m1_sb = sbB("m1_sb", [128, 16, 512], BF16)
    m2_sb = sbB("m2_sb", [128, 4, 8, 512], BF16)
    Mcols = sbB("Mcols", [128, 16], F32)
    Mtmp = sbB("Mtmp", [128, 2], F32)
    Mcols_bf = sbB("Mcols_bf", [128, 16], BF16)
    Mrow = sbB("Mrow", [1, NT], BF16)
    sums_stage = sbB("sums_stage", [65, NT], F32)
    PT = [sbB(f"PT{i}", [128, 1024], BF16) for i in range(2)]
    ystage = [sbB(f"ystage{i}", [64, 512], BF16) for i in range(2)]

    P.dma("sync", lambda sy: sy.dma_start(
        out=m1_sb[:], in_=masks1.rearrange("j p m -> p j m")),
        waits=A_store_waits, inc="sb")
    P.dma("sync", lambda sy: sy.dma_start(
        out=m2_sb[:], in_=masks2.rearrange("a b p m -> p a b m")), inc="sb")

    slot_loaded = {}
    vg_init = {}
    mv_M = {}

    def load_head(h):
        slot = h % NSLOT
        sls = f"sl{slot}"
        wl = P.g(f"slot{slot}")
        P.dma("sync", lambda sy, h=h, slot=slot: sy.dma_start(
            out=Kh[slot][0:64, :], in_=K_scr[h]), waits=wl, inc=sls)
        P.dma("sync", lambda sy, h=h, slot=slot: sy.dma_start(
            out=Kh[slot][64:68, :], in_=biasrows[h]), inc=sls)
        P.dma("sync", lambda sy, h=h, slot=slot: sy.dma_start(
            out=Qh[slot][0:64, :], in_=Q_scr[h]), inc=sls)
        P.dma("sync", lambda sy, slot=slot: sy.dma_start(
            out=Qh[slot][64:68, :], in_=qinit[:]), inc=sls)
        sbv = P.dma("sync", lambda sy, h=h, slot=slot: sy.dma_start(
            out=Vh[slot][:],
            in_=V_scr.rearrange("(kt p) (hh d) -> p kt hh d", p=128, d=65)[:, :, h, :]),
            inc=sls)
        slot_loaded[h] = sbv
        vg_init[h] = sbv

    def p1_gen(h):
        slot = h % NSLOT
        Q_, K_ = Qh[slot], Kh[slot]
        first = True
        vloc = None
        for j in range(16):
            n512 = KTJ[j]
            nch = (n512 + 1) // 2
            for chn in range(nch):
                tiles = list(range(2 * chn, min(2 * chn + 2, n512)))
                t = None
                for idx, kt5 in enumerate(tiles):
                    wl = []
                    if idx == 0:
                        wl += bwait(0, 1)
                        if first:
                            wl += [(f"sl{slot}", slot_loaded[h]), ("sb", 32)] + P.g("Mcols")
                            first = False
                    last = (kt5 == n512 - 1)
                    t = P.op("tensor", lambda e, Q_=Q_, K_=K_, j=j, kt5=kt5,
                             idx=idx, last=last: e.matmul(
                        bank(idx),
                        Q_[0:68, 128 * j:128 * j + 128],
                        K_[0:68, 512 * kt5:512 * kt5 + 512],
                        start=True, stop=not last), waits=wl, inc="t")
                    if last:
                        t = P.op("tensor", lambda e, j=j, idx=idx: e.matmul(
                            bank(idx), id_bf[:], m1_sb[:, j, :],
                            start=False, stop=True), inc="t")
                dst = Mcols[:, j:j + 1] if chn == 0 else Mtmp[:, 0:1]
                wl = [("t", t)]
                if chn > 0:
                    wl.append(("v", vloc))
                vloc = P.op("vector", lambda e, dst=dst, nt_=len(tiles): e.reduce_max(
                    dst, bank(0, n=512 * nt_), axis=AX.X), waits=wl, inc="v")
                for bi in range(len(tiles)):
                    bset(bi, "v", vloc)
                if chn > 0:
                    vloc = P.op("vector", lambda e, j=j: e.tensor_tensor(
                        Mcols[:, j:j + 1], Mcols[:, j:j + 1], Mtmp[:, 0:1], ALU.max),
                        waits=[("v", vloc)], inc="v")
                yield
        v_mbf = P.op("vector", lambda e: e.tensor_copy(Mcols_bf[:], Mcols[:]),
                     waits=[("v", vloc)], inc="v")
        P.setg("Mcols", "v", v_mbf)
        s = None
        for J in range(4):
            for jj in range(4):
                j = 4 * J + jj
                wl = []
                if jj == 0:
                    wl += bwait(7) + [("v", v_mbf)]
                t = P.op("tensor", lambda e, j=j, jj=jj: e.transpose(
                    psum_bf[0:1, 7 * 1024 + 128 * jj:7 * 1024 + 128 * (jj + 1)],
                    Mcols_bf[:, j:j + 1], id_bf[:]), waits=wl, inc="t")
            s = P.op("scalar", lambda e, J=J: e.copy(
                Mrow[0:1, 512 * J:512 * J + 512],
                psum_bf[0:1, 7 * 1024:7 * 1024 + 512]),
                waits=[("t", t)] + (P.g("Mrow") if J == 0 else []), inc="s")
            bset(7, "s", s)
            yield
        mv_M[h] = P.dma("sync", lambda sy, Q_=Q_: sy.dma_start(
            out=Q_[64:65, :], in_=Mrow[0:1, :]), waits=[("s", s)],
            inc=f"sl{slot}")
        P.setg("Mrow", f"sl{slot}", mv_M[h])

    def p2_gen(h):
        slot = h % NSLOT
        Q_, K_, V_ = Qh[slot], Kh[slot], Vh[slot]
        first = True
        pend = None
        t = None

        def emit_pv(p):
            par, kt0, exp_s, J = p
            nkt = 8 * J + 8
            tl = None
            for u in range(2):
                kt = kt0 + u
                wl = [("s", exp_s)] if u == 0 else []
                if kt == 0:
                    wl += bwait(6) + [(f"sl{slot}", vg_init[h])]
                tl = P.op("tensor", lambda e, kt=kt, par=par, u=u, nkt=nkt: e.matmul(
                    bank(6, p1=65),
                    V_[:, kt, :],
                    PT[par][:, 512 * u:512 * u + 512],
                    start=(kt == 0), stop=(kt == nkt - 1)), waits=wl, inc="t")
            P.setg(f"PT{par}", "t", tl)
            return tl

        for J in range(4):
            nkt = 8 * J + 8
            for ktp in range(nkt // 2):
                par = ktp % 2
                b0, b1 = (2, 3) if par == 0 else (4, 5)
                for u, bb in ((0, b0), (1, b1)):
                    kt = 2 * ktp + u
                    masked = kt >= 8 * J
                    wl = []
                    if u == 0:
                        wl += bwait(b0, b1)
                        if first:
                            wl.append((f"sl{slot}", mv_M[h]))
                            first = False
                    t = P.op("tensor", lambda e, K_=K_, Q_=Q_, J=J, kt=kt, bb=bb,
                             masked=masked: e.matmul(
                        bank(bb),
                        K_[0:68, 128 * kt:128 * kt + 128],
                        Q_[0:68, 512 * J:512 * J + 512],
                        start=True, stop=not masked), waits=wl, inc="t")
                    if masked:
                        t = P.op("tensor", lambda e, J=J, kt=kt, bb=bb: e.matmul(
                            bank(bb), id_bf[:], m2_sb[:, J, kt - 8 * J, :],
                            start=False, stop=True), inc="t")
                wl = [("t", t)] + P.g(f"PT{par}")
                s = P.op("scalar", lambda e, par=par, b0=b0: e.activation(
                    PT[par][:], psum[0:128, 512 * b0:512 * b0 + 1024], ACT.Exp),
                    waits=wl, inc="s")
                bset(b0, "s", s)
                bset(b1, "s", s)
                _LAG = True
                if _LAG:
                    if pend is not None:
                        emit_pv(pend)
                    pend = (par, 2 * ktp, s, J)
                else:
                    tl = emit_pv((par, 2 * ktp, s, J))
                yield
            if pend is not None:
                tl = emit_pv(pend)
            pend = None
            par_y = J % 2
            wl = [("t", tl)] + P.g(f"ystage{par_y}")
            vv = P.op("vector", lambda e, par_y=par_y: e.tensor_copy(
                ystage[par_y][:], bank(6, p1=64)), waits=wl, inc="v")
            vv2 = P.op("vector", lambda e, J=J: e.tensor_copy(
                sums_stage[64:65, 512 * J:512 * J + 512], bank(6, p0=64, p1=65)),
                waits=(P.g("sums_stage") if J == 0 else []), inc="v")
            bset(6, "v", vv2)
            ymv = P.dma("sync", lambda sy, h=h, J=J, par_y=par_y: sy.dma_start(
                out=y_scr[:, h, 512 * J:512 * J + 512], in_=ystage[par_y][:]),
                waits=[("v", vv)], inc="yw0" if par_y == 0 else "yw1")
            P.setg(f"ystage{par_y}", "yw0" if par_y == 0 else "yw1", ymv)
            yield
        mvv = P.dma("sync", lambda sy, h=h: sy.dma_start(
            out=sums_all[h:h + 1, :], in_=sums_stage[64:65, :]),
            waits=[("v", vv2)], inc="sw")
        P.setg("sums_stage", "sw", mvv)
        P.setg(f"slot{slot}", "t", P.cnt["t"])

    _PIPELINE = True
    load_head(0)
    load_head(1)
    if _PIPELINE:
        g_prev = None
        for h in range(H):
            if h + 2 < H:
                load_head(h + 2)
            g_cur = p1_gen(h)
            while True:
                try:
                    next(g_cur)
                except StopIteration:
                    g_cur = None
                if g_prev is not None:
                    try:
                        next(g_prev)
                    except StopIteration:
                        g_prev = None
                if g_cur is None and g_prev is None:
                    break
            g_prev = p2_gen(h)
        for _ in g_prev:
            pass
    else:
        for h in range(H):
            if h + 2 < H:
                load_head(h + 2)
            for _ in p1_gen(h):
                pass
            for _ in p2_gen(h):
                pass
    mv_B = P.cnt["sw"]
    yw_B = (P.cnt["yw0"], P.cnt["yw1"])

    P.barrier()
    ctxB.close()

    ctxC = contextlib.ExitStack()

    def sbC(name, shape, dtype):
        return ctxC.enter_context(nc.sbuf_tensor(name, list(shape), dtype))

    Wp_sb = sbC("Wp_sb", [64, 16, C], BF16)
    ytile = [sbC(f"ytile{i}", [64, 16, 128], BF16) for i in range(3)]
    rtile = [sbC(f"rtile{i}", [64, 16, 128], F32) for i in range(3)]
    y_n = [sbC(f"y_n{i}", [64, 16, 128], BF16) for i in range(3)]
    bproj_bc = sbC("bproj_bc", [128, C], F32)
    xo_t = [sbC(f"xo_t{i}", [128, C], F32) for i in range(2)]
    xb_t = sbC("xb_t", [128, C], F32)
    zbuf = sbC("zbuf", [128, 4, C], F32)
    zsq = sbC("zsq", [128, C], F32)
    lnst = sbC("lnst", [128, 24], F32)
    x2t = [sbC(f"x2t{i}", [128, C], F32) for i in range(2)]
    x2bf = [sbC(f"x2bf{i}", [128, C], BF16) for i in range(2)]
    fm_stage = sbC("fm_stage", [128, 8, 128], BF16)

    P.dma("gpsimd", lambda g_: g_.dma_start(
        out=Wp_sb[:], in_=W_proj.rearrange("(kc p) m -> p kc m", p=64)), inc="gc")
    gc = P.dma("gpsimd", lambda g_: g_.dma_start(
        out=bproj_bc[:], in_=b_proj[None, :].broadcast_to([128, C])), inc="gc")
    sr_bproj = gc
    recips = sbC("recips", [16, NT], F32)
    v = P.op("vector", lambda e: e.reciprocal(recips[:], sums_all[:]),
             waits=[("sw", mv_B)], inc="v")
    sr_rec = P.dma("sync", lambda sy: sy.dma_start(
        out=vec_scr[0:16, :], in_=recips[:]), waits=[("v", v)], inc="sr")

    def emit_ln_finalize(ln, producer_v):
        vv = P.op("vector", lambda e: e.tensor_scalar_mul(
            ln[:, 0:4], ln[:, 0:4], 1.0 / C), waits=[("v", producer_v)], inc="v")
        vv = P.op("vector", lambda e: e.tensor_scalar_mul(
            ln[:, 4:8], ln[:, 4:8], 1.0 / C), inc="v")
        vv = P.op("vector", lambda e: e.tensor_tensor(
            ln[:, 8:12], ln[:, 0:4], ln[:, 0:4], ALU.mult), waits=[("v", vv)], inc="v")
        vv = P.op("vector", lambda e: e.tensor_tensor(
            ln[:, 4:8], ln[:, 4:8], ln[:, 8:12], ALU.subtract), waits=[("v", vv)], inc="v")
        vv = P.op("vector", lambda e: e.tensor_scalar_add(
            ln[:, 4:8], ln[:, 4:8], 1e-5), waits=[("v", vv)], inc="v")
        vv = P.op("vector", lambda e: e.reciprocal(
            ln[:, 8:12], ln[:, 4:8]), waits=[("v", vv)], inc="v")
        ss = P.op("scalar", lambda e: e.activation(
            ln[:, 12:16], ln[:, 8:12], ACT.Sqrt), waits=[("v", vv)], inc="s")
        vv = P.op("vector", lambda e: e.tensor_tensor(
            ln[:, 16:20], ln[:, 12:16], ln[:, 12:16], ALU.mult),
            waits=[("s", ss)], inc="v")
        vv = P.op("vector", lambda e: e.tensor_tensor(
            ln[:, 16:20], ln[:, 16:20], ln[:, 4:8], ALU.mult), waits=[("v", vv)], inc="v")
        vv = P.op("vector", lambda e: e.tensor_scalar(
            ln[:, 16:20], ln[:, 16:20], -0.5, 1.5, ALU.mult, ALU.add),
            waits=[("v", vv)], inc="v")
        vv = P.op("vector", lambda e: e.tensor_tensor(
            ln[:, 20:24], ln[:, 12:16], ln[:, 16:20], ALU.mult), waits=[("v", vv)], inc="v")
        return vv

    sx = {0: 0, 1: 0}
    sb_c = P.cnt["sb"]
    pending_tail = []
    tail_state = {"q": 0}
    lnsnap = sbC("lnsnap", [128, 8], F32)

    def emit_tail_tile():
        if not pending_tail:
            return
        tg0, fin_v = pending_tail[0]
        q = tail_state["q"]
        tt = 4 * tg0 + q
        par = tt % 2
        vv = P.op("vector", lambda e, q=q, par=par: e.tensor_scalar(
            x2t[par][:], zbuf[:, q, :], lnsnap[:, q:q + 1], lnsnap[:, 4 + q:5 + q],
            ALU.subtract, ALU.mult), waits=[("v", fin_v)] + P.g(f"x2t{par}"), inc="v")
        P.setg(f"zbuf{q}", "v", vv)
        sa2 = P.dma("sync", lambda sy, tt=tt, par=par: sy.dma_start(
            out=x2_scr[128 * tt:128 * tt + 128, :], in_=x2t[par][:]),
            waits=[("v", vv)], inc=f"xs{par}")
        ss0 = P.op("scalar", lambda e, par=par: e.copy(x2bf[par][:], x2t[par][:]),
                   waits=[("v", vv)] + P.g(f"x2bf{par}"), inc="s")
        tl = None
        for c in range(8):
            wl = []
            if c == 0:
                wl += bwait(6) + [("s", ss0)]
            tl = P.op("tensor", lambda e, c=c, par=par: e.transpose(
                psum_bf[0:128, 6 * 1024 + 128 * c:6 * 1024 + 128 * (c + 1)],
                x2bf[par][:, 128 * c:128 * c + 128], id_bf[:]), waits=wl, inc="t")
        P.setg(f"x2bf{par}", "t", tl)
        ss = P.op("scalar", lambda e: e.copy(
            fm_stage[:], psum_bf[0:128, 6 * 1024:6 * 1024 + 1024]),
            waits=[("t", tl)] + P.g("fm_stage"), inc="s")
        bset(6, "s", ss)
        sa2 = P.dma("sync", lambda sy, tt=tt, par=par: sy.dma_start(
            out=x2fm_scr[:, :, 128 * tt:128 * tt + 128].rearrange("c p m -> p c m"),
            in_=fm_stage[:]), waits=[("s", ss)], inc=f"xs{par}")
        P.setg("fm_stage", f"xs{par}", sa2)
        P.setg(f"x2t{par}", f"xs{par}", sa2)
        if q == 3:
            pending_tail.pop(0)
            tail_state["q"] = 0
        else:
            tail_state["q"] = q + 1

    for tg in range(4):
        for q in range(4):
            tt = 4 * tg + q
            emit_tail_tile()
            par3 = tt % 3
            par = tt % 2
            sxs = "sx0" if par == 0 else "sx1"
            sb_c = P.dma("gpsimd", lambda g_, tt=tt, par3=par3: g_.dma_start(
                out=ytile[par3][:], in_=y_scr[:, :, 128 * tt:128 * tt + 128]),
                waits=P.g(f"ytile{par3}")
                      + ([("yw0", yw_B[0]), ("yw1", yw_B[1])] if tt < 3 else []),
                inc=f"yt{par3}")
            sr = P.dma("gpsimd", lambda g_, tt=tt, par3=par3: g_.dma_start(
                out=rtile[par3][:],
                in_=vec_scr[None, 0:16, 128 * tt:128 * tt + 128]
                .broadcast_to([64, 16, 128])),
                waits=[("sr", sr_rec)] + P.g(f"rtile{par3}"), inc=f"rt{par3}")
            v_yn = P.op("vector", lambda e, par3=par3: e.tensor_tensor(
                y_n[par3][:], ytile[par3][:], rtile[par3][:], ALU.mult),
                waits=[(f"yt{par3}", sb_c), (f"rt{par3}", sr)] + P.g(f"y_n{par3}"),
                inc="v")
            P.setg(f"ytile{par3}", "v", v_yn)
            P.setg(f"rtile{par3}", "v", v_yn)
            sx[par] = P.dma("gpsimd", lambda g_, tt=tt, par=par: g_.dma_start(
                out=xo_t[par][:], in_=x_own[128 * tt:128 * tt + 128, :]),
                waits=P.g(f"xo{par}"), inc=f"xo{par}")
            v_xb = P.op("vector", lambda e, par=par: e.tensor_tensor(
                xb_t[:], xo_t[par][:], bproj_bc[:], ALU.add),
                waits=[(f"xo{par}", sx[par]), ("gc", sr_bproj)] + P.g("xb"), inc="v")
            P.setg(f"xo{par}", "v", v_xb)
            for nh in range(2):
                bk_i = 2 * (tt % 3) + nh
                for kc in range(16):
                    wl = []
                    if kc == 0:
                        wl += bwait(bk_i) + [("v", v_yn)]
                        if tt == 0 and nh == 0:
                            wl += [("gc", gc)]
                    t = P.op("tensor", lambda e, par3=par3, nh=nh, kc=kc, bk_i=bk_i: e.matmul(
                        bank(bk_i),
                        y_n[par3][:, kc, :],
                        Wp_sb[0:64, kc, 512 * nh:512 * nh + 512],
                        start=(kc == 0), stop=(kc == 15)), waits=wl, inc="t")
                v = P.op("vector", lambda e, q=q, nh=nh, bk_i=bk_i: e.tensor_tensor(
                    zbuf[:, q, 512 * nh:512 * nh + 512], bank(bk_i),
                    xb_t[:, 512 * nh:512 * nh + 512], ALU.add),
                    waits=[("t", t)] + ([("v", v_xb)] if nh == 0 else []) +
                          (P.g(f"zbuf{q}") if nh == 0 else []), inc="v")
                bset(bk_i, "v", v)
            P.setg("xb", "v", v)
            P.setg(f"y_n{par3}", "t", t)
            v = P.op("vector", lambda e, q=q: e.reduce_sum(
                lnst[:, q:q + 1], zbuf[:, q, :], axis=AX.X), waits=[("v", v)], inc="v")
            v = P.op("vector", lambda e, q=q: e.tensor_tensor_reduce(
                out=zsq[:], in0=zbuf[:, q, :], in1=zbuf[:, q, :], scale=1.0,
                scalar=0.0, op0=ALU.mult, op1=ALU.add,
                accum_out=lnst[:, 4 + q:5 + q]), waits=[("v", v)], inc="v")
        fin_v = emit_ln_finalize(lnst, v)
        fin_v = P.op("vector", lambda e: e.tensor_copy(lnsnap[:, 0:4], lnst[:, 0:4]),
                     waits=[("v", fin_v)], inc="v")
        fin_v = P.op("vector", lambda e: e.tensor_copy(lnsnap[:, 4:8], lnst[:, 20:24]),
                     waits=[("v", fin_v)], inc="v")
        pending_tail.append((tg, fin_v))
    while pending_tail:
        emit_tail_tile()
    C_store_waits = [("xs0", P.cnt["xs0"]), ("xs1", P.cnt["xs1"])]

    P.barrier()
    ctxC.close()
    ctxBC.close()

    ctxD = contextlib.ExitStack()

    def sbD(name, shape, dtype):
        return ctxD.enter_context(nc.sbuf_tensor(name, list(shape), dtype))

    a_buf = sbD("a_buf", [128, 32, 1024], BF16)
    b_buf = sbD("b_buf", [128, 32, 1024], BF16)
    x2fm_p = sbD("x2fm_p", [128, 8, 1024], BF16)
    w1c = [sbD(f"w1c{i}", [128, 8, 128], BF16) for i in range(2)]
    w2c = [sbD(f"w2c{i}", [128, 8, 128], BF16) for i in range(2)]
    hc = [sbD(f"hc{i}", [128, 16, 128], BF16) for i in range(2)]
    w3c = [sbD(f"w3c{i}", [128, C], BF16) for i in range(2)]
    w3s_bc = sbD("w3s_bc", [128, C], F32)
    silu_t = sbD("silu_t", [128, 1024], F32)
    x2r = [sbD(f"x2r{i}", [128, C], F32) for i in range(2)]
    zbuf2 = sbD("zbuf2", [128, 4, C], F32)
    lnst2 = sbD("lnst2", [128, 24], F32)
    outb = [sbD("outb0", [128, C], F32)]
    zsq2 = silu_t

    sr_w3s = P.dma("sync", lambda sy: sy.dma_start(
        out=w3s_bc[:], in_=w3s[None, :].broadcast_to([128, C])),
        waits=C_store_waits, inc="sr")

    wa = {0: 0, 1: 0}
    wh = {0: 0, 1: 0}
    wc = {0: 0, 1: 0}

    for p in range(2):
        gx = P.dma("sync", lambda sy, p=p: sy.dma_start(
            out=x2fm_p[:], in_=x2fm_scr[:, :, 1024 * p:1024 * p + 1024]
            .rearrange("c p2 m -> p2 c m")),
            waits=C_store_waits + P.g("x2fm_p"), inc="gx")
        glu_v = {}
        evac_s = {}
        for o in range(32):
            par = o % 2
            was = "wa0" if par == 0 else "wa1"
            P.dma("gpsimd", lambda g_, o=o, par=par: g_.dma_start(
                out=w1c[par][:], in_=w1T[:, 128 * o:128 * o + 128]
                .rearrange("(c q2) m -> q2 c m", q2=128)),
                waits=P.g(f"w12c{par}"), inc=was)
            wa[par] = P.dma("gpsimd", lambda g_, o=o, par=par: g_.dma_start(
                out=w2c[par][:], in_=w2T[:, 128 * o:128 * o + 128]
                .rearrange("(c q2) m -> q2 c m", q2=128)), inc=was)
            for ts_ in range(2):
                t_of = {}
                for which, wbuf, bset_i in (("w1", w1c[par], 0), ("w2", w2c[par], 2)):
                    bk_i = bset_i + ts_
                    for c in range(8):
                        wl = []
                        if c == 0:
                            wl += bwait(bk_i) + [(was, wa[par])]
                            if o == 0 and ts_ == 0 and which == "w1":
                                wl.append(("gx", gx))
                        t = P.op("tensor", lambda e, wbuf=wbuf, c=c, ts_=ts_,
                                 bk_i=bk_i: e.matmul(
                            bank(bk_i),
                            wbuf[:, c, :],
                            x2fm_p[:, c, 512 * ts_:512 * ts_ + 512],
                            start=(c == 0), stop=(c == 7)), waits=wl, inc="t")
                    t_of[which] = t
                s = P.op("scalar", lambda e, o=o, ts_=ts_: e.activation(
                    silu_t[:, 512 * ts_:512 * ts_ + 512], bank(ts_), ACT.Silu,
                    scale=scales[:, 0, o:o + 1]),
                    waits=[("t", t_of["w1"])] + P.g(f"silu{ts_}")
                          + (P.g("silu_scr") if o == 0 else []), inc="s")
                bset(ts_, "s", s)
                v = P.op("vector", lambda e, o=o, ts_=ts_: e.scalar_tensor_tensor(
                    a_buf[:, o, 512 * ts_:512 * ts_ + 512], bank(2 + ts_),
                    scales[:, 1, o:o + 1], silu_t[:, 512 * ts_:512 * ts_ + 512],
                    ALU.mult, ALU.mult),
                    waits=[("t", t_of["w2"]), ("s", s)] + (P.g("w3ab") if o == 0 and ts_ == 0 else []),
                    inc="v")
                bset(2 + ts_, "v", v)
                P.setg(f"silu{ts_}", "v", v)
            glu_v_local = v
            P.setg(f"w12c{par}", "t", t_of["w2"])
            glu_v[o] = glu_v_local
        w12_done_v = v
        prev_s = None
        for li, (hT_, src, dstb) in enumerate(((h1T, a_buf, b_buf), (h2T, b_buf, a_buf))):
            scol = 2 + li
            for o in range(32):
                for half in range(2):
                    hidx = 2 * o + half
                    par = hidx % 2
                    whs = "wh0" if par == 0 else "wh1"
                    wh[par] = P.dma("gpsimd", lambda g_, hT_=hT_, o=o, par=par,
                                    half=half: g_.dma_start(
                        out=hc[par][:],
                        in_=hT_[2048 * half:2048 * half + 2048, 128 * o:128 * o + 128]
                        .rearrange("(c q2) m -> q2 c m", q2=128)),
                        waits=P.g(f"hc{par}"), inc=whs)
                    for ts_ in range(2):
                        bk_i = 4 + ts_
                        for c in range(16):
                            cglob = 16 * half + c
                            wl = []
                            if c == 0:
                                wl.append((whs, wh[par]))
                                if half == 0:
                                    wl += bwait(bk_i)
                            if ts_ == 0 and o == 0:
                                if li == 0:
                                    wl.append(("v", glu_v[cglob]))
                                else:
                                    wl.append(("s", evac_s[(0, cglob)]))
                            t = P.op("tensor", lambda e, src=src, par=par, c=c, ts_=ts_,
                                     half=half, bk_i=bk_i: e.matmul(
                                bank(bk_i),
                                hc[par][:, c, :],
                                src[:, 16 * half + c, 512 * ts_:512 * ts_ + 512],
                                start=(half == 0 and c == 0),
                                stop=(half == 1 and c == 15)), waits=wl, inc="t")
                    P.setg(f"hc{par}", "t", t)
                for ts_ in range(2):
                    bk_i = 4 + ts_
                    s = P.op("scalar", lambda e, dstb=dstb, o=o, ts_=ts_, bk_i=bk_i,
                             scol=scol: e.activation(
                        dstb[:, o, 512 * ts_:512 * ts_ + 512], bank(bk_i), ACT.Gelu,
                        scale=scales[:, scol, o:o + 1]), waits=[("t", t)], inc="s")
                    bset(bk_i, "s", s)
                evac_s[(li, o)] = s
            prev_s = s
        h2_done_s = s
        if p == 0:
            v = P.op("vector", lambda e: e.memset(lnst2[:], 0.5), inc="v")
        for tg_ in range(2):
            sx_of_q = {}
            for q in range(2):
                tt = 8 * p + 4 * tg_ + q
                par = q % 2
                sxs = "sx0" if par == 0 else "sx1"
                sx_of_q[q] = P.dma("sync", lambda sy, tt=tt, par=par: sy.dma_start(
                    out=x2r[par][:], in_=x2_scr[128 * tt:128 * tt + 128, :]),
                    waits=P.g(f"x2r{par}"), inc=sxs)
            for c in range(32):
                par = c % 2
                wcs = "wc0" if par == 0 else "wc1"
                wc[par] = P.dma("gpsimd", lambda g_, c=c, par=par: g_.dma_start(
                    out=w3c[par][:], in_=w3T[128 * c:128 * c + 128, :]),
                    waits=P.g(f"w3c{par}"), inc=wcs)
                for q in range(4):
                    for nh in range(2):
                        bk_i = 2 * q + nh
                        wl = []
                        if q == 0 and nh == 0:
                            wl.append((wcs, wc[par]))
                            if tg_ == 0:
                                wl.append(("s", evac_s[(1, c)]))
                        if c == 0:
                            wl += bwait(bk_i)
                        t = P.op("tensor", lambda e, c=c, q=q, nh=nh, tg_=tg_,
                                 par=par, bk_i=bk_i: e.matmul(
                            bank(bk_i),
                            a_buf[:, c, 512 * tg_ + 128 * q:512 * tg_ + 128 * q + 128],
                            w3c[par][:, 512 * nh:512 * nh + 512],
                            start=(c == 0), stop=(c == 31)), waits=wl, inc="t")
                P.setg(f"w3c{par}", "t", t)
            t_w3 = t
            for q in range(4):
                par = q % 2
                sxs = "sx0" if par == 0 else "sx1"
                for nh in range(2):
                    bk_i = 2 * q + nh
                    wl = []
                    if nh == 0:
                        wl = [("t", t_w3), (sxs, sx_of_q[q])] + P.g(f"zb2_{q}")
                    v = P.op("vector", lambda e, q=q, nh=nh, bk_i=bk_i, par=par: e.scalar_tensor_tensor(
                        zbuf2[:, q, 512 * nh:512 * nh + 512], bank(bk_i),
                        1.0, w3s_bc[:, 512 * nh:512 * nh + 512],
                        ALU.mult, ALU.mult), waits=wl + ([("sr", sr_w3s)] if q == 0 and nh == 0 else []), inc="v")
                    v = P.op("vector", lambda e, q=q, nh=nh, par=par: e.tensor_tensor(
                        zbuf2[:, q, 512 * nh:512 * nh + 512],
                        zbuf2[:, q, 512 * nh:512 * nh + 512],
                        x2r[par][:, 512 * nh:512 * nh + 512], ALU.add),
                        waits=[("v", v)], inc="v")
                    bset(bk_i, "v", v)
                P.setg(f"x2r{par}", "v", v)
                if q + 2 <= 3:
                    q2 = q + 2
                    tt2 = 8 * p + 4 * tg_ + q2
                    sx_of_q[q2] = P.dma("sync", lambda sy, tt2=tt2, par=par: sy.dma_start(
                        out=x2r[par][:], in_=x2_scr[128 * tt2:128 * tt2 + 128, :]),
                        waits=[("v", v)], inc=sxs)
                v = P.op("vector", lambda e, q=q: e.reduce_sum(
                    lnst2[:, q:q + 1], zbuf2[:, q, :], axis=AX.X), waits=[("v", v)], inc="v")
                v = P.op("vector", lambda e, q=q: e.tensor_tensor_reduce(
                    out=zsq2[:], in0=zbuf2[:, q, :], in1=zbuf2[:, q, :], scale=1.0,
                    scalar=0.0, op0=ALU.mult, op1=ALU.add,
                    accum_out=lnst2[:, 4 + q:5 + q]), waits=[("v", v)], inc="v")
                P.setg("silu_scr", "v", v)
            v = emit_ln_finalize(lnst2, v)
            for q in range(4):
                tt = 8 * p + 4 * tg_ + q
                v = P.op("vector", lambda e, q=q: e.tensor_scalar(
                    outb[0][:], zbuf2[:, q, :], lnst2[:, q:q + 1],
                    lnst2[:, 20 + q:21 + q], ALU.subtract, ALU.mult),
                    waits=[("v", v)] + P.g("outb0"), inc="v")
                P.setg(f"zb2_{q}", "v", v)
                oc_ = P.dma("sync", lambda sy, tt=tt: sy.dma_start(
                    out=out_h[128 * tt:128 * tt + 128, :], in_=outb[0][:]),
                    waits=[("v", v)], inc="out")
                P.setg("outb0", "out", oc_)
        P.setg("x2fm_p", "t", P.cnt["t"])

    P.wait("sync", [("out", P.cnt["out"])])

    with nc.Block() as block:
        with contextlib.ExitStack() as es:
            sems = {sname: es.enter_context(nc.semaphore(sname)) for sname in SEMS}
            P.emit(block, sems)
    ctxD.close()
    ctx.close()
    mybir.codegen_inst_isa_subclasses(nc)
    return nc


_slope0 = 2.0 ** (-2.0 ** (-(math.log2(H) - 3)))
_SLOPES = [_slope0 * _slope0 ** i for i in range(H)]


def _host_consts(half):
    subs = SUBS[half]
    m1 = np.zeros((16, 128, 512), dtype=np.float32)
    for j in range(16):
        g = subs[j // 2]
        qp = 256 * g + 128 * (j % 2)
        kbase = 512 * (KTJ[j] - 1)
        kk = kbase + np.arange(512)[None, :]
        qq = qp + np.arange(128)[:, None]
        m1[j] = np.where(kk <= qq, 0.0, NEG)
    m2 = np.zeros((4, 8, 128, 512), dtype=np.float32)
    for J in range(4):
        qglob = np.empty(512, dtype=np.int64)
        for hf in range(2):
            g = subs[2 * J + hf]
            qglob[256 * hf:256 * hf + 256] = 256 * g + np.arange(256)
        for ti in range(8):
            kt = 8 * J + ti
            kk = 128 * kt + np.arange(128)[:, None]
            m2[J, ti] = np.where(kk <= qglob[None, :], 0.0, NEG)
    return m1.astype(bf16), m2.astype(bf16)


def _bias_rows():
    br = np.zeros((H, 4, T), dtype=bf16)
    br[:, 0, :] = bf16(-1.0)
    k = np.arange(T, dtype=np.float64)
    for h in range(H):
        b = _SLOPES[h] * k
        b1 = b.astype(np.float32).astype(bf16)
        r = b - b1.astype(np.float64)
        b2 = r.astype(np.float32).astype(bf16)
        r = r - b2.astype(np.float64)
        b3 = r.astype(np.float32).astype(bf16)
        br[h, 1], br[h, 2], br[h, 3] = b1, b2, b3
    return br


_NC = None


def kernel(L0_attn_x, L0_attn_W_attn, L0_attn_b_attn, L0_attn_W_proj,
           L0_attn_b_proj, L0_attn_temperature, L0_ffn_w1_w, L0_ffn_w1_s,
           L0_ffn_w2_w, L0_ffn_w2_s, L0_ffn_h1_w, L0_ffn_h1_s, L0_ffn_h2_w,
           L0_ffn_h2_s, L0_ffn_w3_w, L0_ffn_w3_s):
    global _NC
    if _NC is None:
        _NC = _build()
    nc = _NC

    x = np.asarray(L0_attn_x, dtype=np.float32)
    shared = {
        "W_attn": np.asarray(L0_attn_W_attn, np.float32),
        "b_attn": np.asarray(L0_attn_b_attn, np.float32),
        "W_proj": np.asarray(L0_attn_W_proj, np.float32),
        "b_proj": np.asarray(L0_attn_b_proj, np.float32),
        "temperature": np.asarray(L0_attn_temperature, np.float32),
        "w1T": np.ascontiguousarray(np.asarray(L0_ffn_w1_w, np.float32).T),
        "w2T": np.ascontiguousarray(np.asarray(L0_ffn_w2_w, np.float32).T),
        "h1T": np.ascontiguousarray(np.asarray(L0_ffn_h1_w, np.float32).T),
        "h2T": np.ascontiguousarray(np.asarray(L0_ffn_h2_w, np.float32).T),
        "w3T": np.ascontiguousarray(np.asarray(L0_ffn_w3_w, np.float32).T),
        "w3s": np.asarray(L0_ffn_w3_s, np.float32),
        "scales_in": np.ascontiguousarray(np.stack(
            [np.asarray(a, np.float32).reshape(32, 128).T
             for a in (L0_ffn_w1_s, L0_ffn_w2_s, L0_ffn_h1_s, L0_ffn_h2_s)], axis=1)),
        "bqk_in": np.ascontiguousarray(np.stack(
            [np.asarray(L0_attn_b_attn[0:1024], np.float32).reshape(8, 128).T,
             np.asarray(L0_attn_b_attn[1024:2048], np.float32).reshape(8, 128).T],
            axis=1)),
        "ident": np.eye(128, dtype=bf16),
        "biasrows": _bias_rows(),
        "qinit": np.concatenate([np.zeros((1, NT), bf16), np.ones((3, NT), bf16)]),
    }
    consts_h = [_host_consts(0), _host_consts(1)]

    in_maps = []
    for c in range(8):
        b, half = c // 2, c % 2
        subs = SUBS[half]
        xo = np.concatenate([x[b, 256 * g:256 * g + 256, :] for g in subs], axis=0)
        m1, m2 = consts_h[half]
        im = dict(shared)
        im["x_own"] = np.ascontiguousarray(xo)
        im["xT_own"] = np.ascontiguousarray(xo.T)
        im["xT_b"] = np.ascontiguousarray(x[b].T)
        im["masks1"] = m1
        im["masks2"] = m2
        in_maps.append(im)

    res = run_bass_kernel_spmd(nc, in_maps, core_ids=list(range(8)))

    out = np.empty((B, T, C), dtype=np.float32)
    for c in range(8):
        b, half = c // 2, c % 2
        subs = SUBS[half]
        co = res.results[c]["out"]
        for idx, g in enumerate(subs):
            out[b, 256 * g:256 * g + 256, :] = co[256 * idx:256 * idx + 256, :]
    return out
